# version 17
# baseline (speedup 1.0000x reference)
import math
import contextlib
import numpy as np
import ml_dtypes
import concourse.bass as bass
import concourse.mybir as mybir
from concourse.bass_utils import run_bass_kernel_spmd

F32 = mybir.dt.float32
BF16 = mybir.dt.bfloat16
I32 = mybir.dt.int32
AF = mybir.ActivationFunctionType
ALU = mybir.AluOpType
AX = mybir.AxisListType

D = 4096
T = 2048
DC = 32
TC = 16
NQT = 4
QW = 512
NH = 32
NKV = 8
GQ = 4
DH = 128
E = 8
R = 16
WIN = 1024
SM = DH ** -0.5
EXPC = 12.0
EPS = 1e-5
THETA = 10000.0
TWO_PI_HI = float(np.float32(2.0 * math.pi))
TWO_PI_LO = float(np.float32(2.0 * math.pi - np.float32(2.0 * math.pi)))
INV_2PI = float(np.float32(1.0 / (2.0 * math.pi)))

PHASES = (1, 2, 3)

ENGS = ('tn', 've', 'sc', 'gp', 'sy')
ENG_SEMS = ('tn', 've', 'sc', 'gp')
QUEUES = {'dgp': 6, 'dsy': 6, 'dsc': 4}
LANE_SEMS = tuple(f"{q}{i}" for q, n in QUEUES.items() for i in range(n))
ALL_SEMS = ENG_SEMS + LANE_SEMS


class Sch:
    def __init__(self):
        self.ops = {e: [] for e in ENGS}
        self.cnt = {s: 0 for s in ALL_SEMS}
        self.hwm = {e: {} for e in ENGS}
        self.rot = {q: 0 for q in QUEUES}

    def cur(self, sem):
        return self.cnt[sem]

    def op(self, eng, fn, waits=(), inc=None):
        wl = list(waits)
        if inc in QUEUES:
            lane_i = self.rot[inc]
            self.rot[inc] = (lane_i + 1) % QUEUES[inc]
            lane = f"{inc}{lane_i}"
            wl.append((lane, self.cnt[lane]))
            inc = lane
            amt = 16
        else:
            amt = 1
        fl = []
        h = self.hwm[eng]
        for (s, v) in wl:
            if v <= 0:
                continue
            if s == eng:
                continue
            if h.get(s, 0) >= v:
                continue
            fl.append((s, v))
            h[s] = v
        if inc is not None:
            self.cnt[inc] += amt
        self.ops[eng].append((fn, tuple(fl), inc))
        if inc is not None:
            return (inc, self.cnt[inc])
        return None

    def qall(self, q):
        return [(f"{q}{i}", self.cnt[f"{q}{i}"]) for i in range(QUEUES[q])
                if self.cnt[f"{q}{i}"] > 0]

    def barrier_wait(self, eng, waits):
        self.op(eng, None, waits=waits, inc=None)

    def snapshot(self):
        return [(k, self.cnt[k]) for k in ALL_SEMS if self.cnt[k] > 0]

    def phase_barrier(self):
        snap = self.snapshot()
        for e in ENGS:
            self.barrier_wait(e, snap)


def _emit(block, sch, sems):
    def mk(name):
        autodrain = name in ('ve', 'gp')

        def f(e):
            for (fn, waits, inc) in sch.ops[name]:
                for (s, v) in waits:
                    e.wait_ge(sems[s], v)
                if fn is None:
                    continue
                if autodrain:
                    e.drain()
                inst = fn(e)
                if inc is not None:
                    amt = 16 if inc in LANE_SEMS else 1
                    inst.then_inc(sems[inc], amt)
        return f
    block.tensor(mk('tn'))
    block.vector(mk('ve'))
    block.scalar(mk('sc'))
    block.gpsimd(mk('gp'))
    block.sync(mk('sy'))


class Builder:
    def __init__(self, nc, sch, hd):
        self.nc = nc
        self.s = sch
        self.h = hd

    def bfslice(self, off, cols, rows=128):
        return self.h['SCR_BF'][0:rows, off:off + cols]

    def f32slice(self, off, cols, rows=128):
        return self.h['SCR_F32'][0:rows, off:off + cols]


def build(phases=PHASES):
    nc = bass.Bass("TRN2", target_bir_lowering=False)
    dt = {}
    dt['x'] = nc.dram_tensor("x", [T, D], F32, kind="ExternalInput").ap()
    dt['wq'] = nc.dram_tensor("wq", [D, D], F32, kind="ExternalInput").ap()
    dt['wk'] = nc.dram_tensor("wk", [D, NKV * DH], F32, kind="ExternalInput").ap()
    dt['wv'] = nc.dram_tensor("wv", [D, NKV * DH], F32, kind="ExternalInput").ap()
    dt['wo'] = nc.dram_tensor("wo", [D, D], F32, kind="ExternalInput").ap()
    dt['bw'] = nc.dram_tensor("bw", [D, D], F32, kind="ExternalInput").ap()
    dt['rw'] = nc.dram_tensor("rw", [E, D], F32, kind="ExternalInput").ap()
    dt['la'] = nc.dram_tensor("la", [E, D, R], F32, kind="ExternalInput").ap()
    dt['lb'] = nc.dram_tensor("lb", [E, R, D], F32, kind="ExternalInput").ap()
    dt['out'] = nc.dram_tensor("out", [T, D], F32, kind="ExternalOutput").ap()
    dt['s_x0'] = nc.dram_tensor("s_x0", [T, D], BF16, kind="Internal").ap()
    dt['s_bwt'] = nc.dram_tensor("s_bwt", [D, D], BF16, kind="Internal").ap()
    dt['s_x1'] = nc.dram_tensor("s_x1", [D, T], BF16, kind="Internal").ap()
    dt['s_fin'] = nc.dram_tensor("s_fin", [D, T], BF16, kind="Internal").ap()
    dt['s_q'] = nc.dram_tensor("s_q", [NH * DH, T], BF16, kind="Internal").ap()
    dt['s_k'] = nc.dram_tensor("s_k", [NKV * DH, T], BF16, kind="Internal").ap()
    dt['s_v'] = nc.dram_tensor("s_v", [NKV * DH, T], BF16, kind="Internal").ap()
    dt['s_vt'] = nc.dram_tensor("s_vt", [T, NKV * DH], BF16, kind="Internal").ap()
    dt['s_wo'] = nc.dram_tensor("s_wo", [D, D], BF16, kind="Internal").ap()

    with contextlib.ExitStack() as ctx:
        En = ctx.enter_context
        h = {}
        h['X'] = En(nc.sbuf_tensor("X", [128, DC * T], BF16))
        h['WB0'] = En(nc.sbuf_tensor("WB0", [128, 4096], BF16))
        h['WB1'] = En(nc.sbuf_tensor("WB1", [128, 4096], BF16))
        h['IOTA'] = En(nc.sbuf_tensor("IOTA", [128, 128], F32))
        h['IDENT'] = En(nc.sbuf_tensor("IDENT", [128, 128], BF16))
        h['ONES'] = En(nc.sbuf_tensor("ONESM", [128, 128], BF16))
        h['PSW'] = En(nc.sbuf_tensor("PSW", [128, 128], BF16))
        h['ONEC'] = En(nc.sbuf_tensor("ONEC", [128, 1], BF16))
        h['E2'] = En(nc.sbuf_tensor("E2", [8, 128], BF16))
        h['E2F'] = En(nc.sbuf_tensor("E2F", [8, 128], F32))
        h['RWFM'] = En(nc.sbuf_tensor("RWFM", [128, DC * E], BF16))
        h['COLW'] = En(nc.sbuf_tensor("COLW", [128, 2], F32))
        h['COLWA'] = En(nc.sbuf_tensor("COLWA", [128, DC], F32))
        h['COLA'] = En(nc.sbuf_tensor("COLA", [128, 1], F32))
        h['COLH0'] = En(nc.sbuf_tensor("COLH0", [128, 1], F32))
        h['COLH1'] = En(nc.sbuf_tensor("COLH1", [128, 1], F32))
        h['COLRW'] = En(nc.sbuf_tensor("COLRW", [8, 1], F32))
        h['NEGM1'] = En(nc.sbuf_tensor("NEGM1", [128, 1], F32))
        h['NEGM2'] = En(nc.sbuf_tensor("NEGM2", [128, 1], F32))
        h['RDEN'] = En(nc.sbuf_tensor("RDEN", [128, 1], F32))
        h['TM8A'] = En(nc.sbuf_tensor("TM8A", [128, 8], F32))
        h['TM8B'] = En(nc.sbuf_tensor("TM8B", [128, 8], F32))
        h['TM8C'] = En(nc.sbuf_tensor("TM8C", [128, 8], F32))
        h['WTM'] = En(nc.sbuf_tensor("WTM", [128, 8], BF16))
        h['PCOL'] = En(nc.sbuf_tensor("PCOL", [128, 1], F32))
        h['PCOL2'] = En(nc.sbuf_tensor("PCOL2", [128, 1], F32))
        h['INVF'] = En(nc.sbuf_tensor("INVF", [128, 1], F32))
        h['CEXP'] = En(nc.sbuf_tensor("CEXP", [128, 1], F32))
        h['MASK1'] = En(nc.sbuf_tensor("MASK1", [128, 128], BF16))
        h['MASK2'] = En(nc.sbuf_tensor("MASK2", [128, 128], BF16))
        h['SCR_BF'] = En(nc.sbuf_tensor("SCR_BF", [128, 25088], BF16))
        h['SCR_F32'] = En(nc.sbuf_tensor("SCR_F32", [128, 2048], F32))
        h['SCR_I32'] = En(nc.sbuf_tensor("SCR_I32", [128, 512], I32))
        PS = [En(nc.psum_tensor(f"ps{i}", [128, 512], F32)) for i in range(7)]
        PSB = En(nc.psum_tensor("psb", [128, 1024], BF16))
        sems = {s: En(nc.semaphore(s)) for s in ALL_SEMS}
        block = En(nc.Block())

        sch = Sch()
        b = Builder(nc, sch, h)
        b.PS = PS
        b.PSB = PSB
        b.dt = dt
        emit_program(b, phases)
        _emit(block, sch, sems)
    return nc


def emit_program(b, phases):
    emit_startup(b)
    if 1 in phases:
        emit_ffn_phase(b, dst='s_x1', barrier=False)
        emit_x_reload(b, 's_x1')
    if 2 in phases:
        emit_attn_phase(b)
    if 3 in phases:
        emit_ffn_phase(b, dst='s_fin', th_outer=True)
        emit_output(b, 's_fin', tcs=range(8, TC))
        b.s.barrier_wait('gp', b.s.qall('dgp'))
    elif 1 in phases and 2 not in phases:
        emit_output(b, 's_x1')
    elif 2 in phases:
        emit_x_dump(b, 's_fin')
        emit_output(b, 's_fin')


def emit_startup(b):
    s, h, dt = b.s, b.h, b.dt
    X, WB = h['X'], (h['WB0'], h['WB1'])

    s.op('gp', lambda e: e.iota(h['IOTA'][:], pattern=[[1, 128]], base=0,
                                channel_multiplier=-1,
                                allow_small_or_imprecise_dtypes=True), inc='gp')
    g_iota = s.cur('gp')
    s.op('gp', lambda e: e.memset(h['ONES'][:], 1.0), inc='gp')
    s.op('gp', lambda e: e.memset(h['ONEC'][:], 1.0), inc='gp')
    s.op('gp', lambda e: e.memset(h['CEXP'][:], -EXPC), inc='gp')
    b.g_cexp = s.cur('gp')
    s.op('gp', lambda e: e.iota(h['E2F'][:], pattern=[[1, 8], [0, 16]], base=0,
                                channel_multiplier=-1,
                                allow_small_or_imprecise_dtypes=True), inc='gp')
    g_e2 = s.cur('gp')
    s.op('gp', lambda e: e.iota(h['PCOL'][:], pattern=[[0, 1]], base=0,
                                channel_multiplier=1,
                                allow_small_or_imprecise_dtypes=True), inc='gp')
    b.g_pcol = s.cur('gp')
    s.op('ve', lambda e: e.tensor_single_scalar(h['IDENT'][:], h['IOTA'][:, 0:128],
                                                0.0, ALU.is_equal),
         waits=[('gp', g_iota)], inc='ve')
    s.op('ve', lambda e: e.tensor_single_scalar(h['PSW'][0:64, :], h['IOTA'][0:64, 0:128],
                                                64.0, ALU.is_equal), inc='ve')
    s.op('ve', lambda e: e.tensor_single_scalar(h['PSW'][64:128, :], h['IOTA'][64:128, 0:128],
                                                -64.0, ALU.is_equal), inc='ve')
    s.op('ve', lambda e: e.tensor_single_scalar(h['E2F'][:], h['E2F'][:],
                                                0.0, ALU.is_equal),
         waits=[('gp', g_e2)], inc='ve')
    s.op('ve', lambda e: e.tensor_single_scalar(h['E2'][:], h['E2F'][:],
                                                -2.0, ALU.mult), inc='ve')
    b.v_const = s.cur('ve')

    XSTG = (20480, 8192)
    xst_free = [[], []]
    psb_bank_free = [0]
    X_r = X.ap().rearrange("p (dc t) -> p dc t", t=T)
    for tc in range(TC):
        base = XSTG[tc % 2]
        tok = s.op('gp', (lambda tc=tc, base=base: lambda e: e.dma_start(
            out=b.bfslice(base, 4096), in_=dt['x'][tc * 128:(tc + 1) * 128, :]))(),
            waits=xst_free[tc % 2], inc='dgp')
        for g in range(4):
            for slot in range(8):
                dc = g * 8 + slot
                s.op('tn', (lambda dc=dc, slot=slot, base=base: lambda e: e.transpose(
                    b.PSB[:, slot * 128:(slot + 1) * 128],
                    b.bfslice(base + dc * 128, 128),
                    h['IDENT'][:]))(),
                    waits=[tok, ('ve', b.v_const), ('ve', psb_bank_free[0])],
                    inc='tn' if slot == 7 else None)
            tv = s.cur('tn')
            s.op('ve', (lambda tc=tc, g=g, tv=tv, X_r=X_r: lambda e: e.tensor_copy(
                X_r[:, g * 8:(g + 1) * 8, tc * 128:(tc + 1) * 128],
                b.PSB.ap().rearrange("p (slot j) -> p slot j", j=128)))(),
                waits=[('tn', tv)], inc='ve')
            psb_bank_free[0] = s.cur('ve')
        xst_free[tc % 2] = [('tn', s.cur('tn'))]
    b.x_ready = [('ve', s.cur('ve'))]
    b.psb_free = ('ve', s.cur('ve'))
    b.xstage_free = xst_free[0] + xst_free[1]

    def _prep():
        for oc in range(DC):
            s.op('gp', (lambda oc=oc: lambda e: e.dma_start(
                out=dt['s_wo'][oc * 128:(oc + 1) * 128, :],
                in_=dt['wo'][oc * 128:(oc + 1) * 128, :]))(), inc='dgp')
        b.wo_prep = s.qall('dgp')
        b.deferred_prep = None
        b.fuse_bw_prep = True
    b.deferred_prep = _prep
    b.bwstage_free = []
    b.wb_free = [[], []]
    b.bwt_done = []
    b.wo_prep = []


def emit_x_reload(b, src):
    s, h, dt = b.s, b.h, b.dt
    waits = list(getattr(b, 'x_consumers', [])) + list(b.fin_stored)
    toks = []
    for dc in range(DC):
        toks.append(s.op('sy', (lambda dc=dc: lambda e: e.dma_start(
            out=h['X'][:, dc * T:(dc + 1) * T],
            in_=dt[src][dc * 128:(dc + 1) * 128, :]))(),
            waits=waits, inc='dsy'))
    b.x_ready = toks
    b.x_consumers = []


def emit_x_dump(b, dst):
    s, h, dt = b.s, b.h, b.dt
    s.phase_barrier()
    toks = []
    for dc in range(DC):
        toks.append(s.op('sy', (lambda dc=dc: lambda e: e.dma_start(
            out=dt[dst][dc * 128:(dc + 1) * 128, :],
            in_=h['X'][:, dc * T:(dc + 1) * T]))(), inc='dsy'))
    b.fin_stored = toks


def emit_stats(b, rsig_o, nmsh_o, mean_o, xsq_off):
    s, h = b.s, b.h
    X = b.h['X']
    stage_w = list(getattr(b, 'bwstage_free', []))
    b.bwstage_free = []
    t1 = b.f32slice(0, 512)
    t2 = b.f32slice(512, 512)
    xsq = (b.bfslice(xsq_off, 1024), b.bfslice(xsq_off + 1024, 1024))
    tn_xsq_read = [0, 0]
    for half in range(2):
        co = half * 1024
        sc_sq = [0, 0]
        for dc in range(DC):
            xb = xsq[dc % 2]
            w = list(b.x_ready) + [('tn', tn_xsq_read[dc % 2])] + stage_w
            s.op('sc', (lambda xb=xb, dc=dc, co=co: lambda e: e.activation(
                xb[:], X[:, dc * T + co:dc * T + co + 1024], AF.Square))(),
                waits=w, inc='sc')
            sc_sq[dc % 2] = s.cur('sc')
            for qi in range(2):
                qo = co + qi * 512
                s.op('tn', (lambda dc=dc, qi=qi, qo=qo: lambda e: e.matmul(
                    b.PS[qi][:], h['ONES'][:], X[:, dc * T + qo:dc * T + qo + 512],
                    start=(dc == 0), stop=(dc == DC - 1)))(),
                    waits=[('ve', b.v_const), ('ve', getattr(b, 'ps_free_stats', 0))]
                          + list(b.x_ready),
                    inc=None)
                s.op('tn', (lambda xb=xb, qi=qi, co2=qi * 512, dc=dc:
                            lambda e: e.matmul(
                    b.PS[2 + qi][:], h['ONES'][:], xb[:, co2:co2 + 512],
                    start=(dc == 0), stop=(dc == DC - 1)))(),
                    waits=[('sc', sc_sq[dc % 2])],
                    inc='tn' if qi == 1 else None)
            tn_xsq_read[dc % 2] = s.cur('tn')
        tn_stats = s.cur('tn')
        for qi in range(2):
            qo = co + qi * 512
            mean_sl = b.bfslice(mean_o + qo, 512)
            s.op('ve', (lambda qi=qi, mean_sl=mean_sl: lambda e: e.tensor_single_scalar(
                mean_sl, b.PS[qi][:], 1.0 / D, ALU.mult))(),
                waits=[('tn', tn_stats)] + stage_w, inc='ve')
            s.op('ve', (lambda mean_sl=mean_sl: lambda e: e.tensor_tensor(
                t1[:], mean_sl, mean_sl, ALU.mult))(), inc='ve')
            s.op('ve', (lambda qi=qi: lambda e: e.scalar_tensor_tensor(
                t2[:], b.PS[2 + qi][:], 1.0 / D, t1[:], ALU.mult, ALU.subtract))(),
                inc='ve')
            s.op('ve', (lambda: lambda e: e.tensor_single_scalar(
                t2[:], t2[:], EPS, ALU.add))(), inc='ve')
            v_var = s.cur('ve')
            s.op('sc', (lambda: lambda e: e.activation(t1[:], t2[:], AF.Sqrt))(),
                waits=[('ve', v_var)], inc='sc')
            sc_sqrt = s.cur('sc')
            s.op('ve', (lambda: lambda e: e.reciprocal(t2[:], t1[:]))(),
                waits=[('sc', sc_sqrt)], inc='ve')
            s.op('ve', (lambda qo=qo: lambda e: e.tensor_single_scalar(
                b.bfslice(rsig_o + qo, 512), t2[:], 1.0, ALU.mult))(), inc='ve')
            s.op('ve', (lambda qo=qo, mean_sl=mean_sl: lambda e: e.scalar_tensor_tensor(
                b.bfslice(nmsh_o + qo, 512), mean_sl, -1.0, t2[:], ALU.mult, ALU.mult))(),
                inc='ve')
        b.ps_free_stats = s.cur('ve')
    b.v_stats = s.cur('ve')
    b.sc_stats = s.cur('sc')


def emit_ffn_phase(b, dst, barrier=True, th_outer=False):
    s, h, dt = b.s, b.h, b.dt
    if barrier:
        s.phase_barrier()
    X = h['X']
    A_RSIG, A_NMSH, A_MEAN = 0, 2048, 4096
    A_XSQ = 6144
    A_AFM = 8192
    A_BFM = 12288
    A_MIDW = 16384
    A_WX2 = 18432
    A_LOG = 20480
    A_FIN = 22528
    rsig = lambda qo, n=512: b.bfslice(A_RSIG + qo, n)
    nmsh = lambda qo, n=512: b.bfslice(A_NMSH + qo, n)
    mean = lambda qo, n=512: b.bfslice(A_MEAN + qo, n)

    RWN = b.bfslice(A_MIDW, 4096, rows=8)
    g_rwn = s.op('gp', (lambda RWN=RWN: lambda e: e.dma_start(
        out=RWN, in_=dt['rw'][:]))(), inc='dgp')
    xw = list(getattr(b, 'xstage_free', []))
    b.xstage_free = []
    g_bfm = s.op('gp', lambda e: e.dma_start(
        out=b.bfslice(A_BFM, 4096), in_=dt['lb'].rearrange("e r d -> (e r) d")),
        waits=xw, inc='dgp')
    la_r = dt['la'].rearrange("e (dc p) r -> dc p e r", p=128)
    afm_toks = []
    for dc in range(DC):
        afm_toks.append(s.op('gp', (lambda dc=dc: lambda e: e.dma_start(
            out=h['SCR_BF'].ap().rearrange("p (c e r) -> p c e r", e=8, r=16)[:, (A_AFM // 128) + dc],
            in_=la_r[dc]))(), waits=xw, inc='dgp'))
    emit_stats(b, A_RSIG, A_NMSH, A_MEAN, A_XSQ)

    for dc in range(DC):
        s.op('tn', (lambda dc=dc: lambda e: e.transpose(
            b.PSB[:, dc % 2 * 8: dc % 2 * 8 + 8],
            b.bfslice(A_MIDW + dc * 128, 128, rows=8),
            h['IDENT'][0:8, 0:8]))(),
            waits=[g_rwn, ('ve', b.v_const), ('ve', s.cur('ve') if dc >= 2 else 0),
                   getattr(b, 'psb_free', ('ve', 0))],
            inc='tn')
        tv = s.cur('tn')
        s.op('ve', (lambda dc=dc, tv=tv: lambda e: e.tensor_copy(
            h['RWFM'][:, dc * 8:(dc + 1) * 8], b.PSB[:, dc % 2 * 8: dc % 2 * 8 + 8]))(),
            waits=[('tn', tv)], inc='ve')
    v_rwfm = s.cur('ve')
    for dc in range(DC):
        s.op('tn', (lambda dc=dc: lambda e: e.matmul(
            b.PS[5][0:8, 0:1], h['RWFM'][:, dc * 8:(dc + 1) * 8], h['ONEC'][:],
            start=(dc == 0), stop=(dc == DC - 1)))(),
            waits=[('ve', v_rwfm)], inc='tn' if dc == DC - 1 else None)
    s.op('ve', lambda e: e.tensor_copy(h['COLRW'][:], b.PS[5][0:8, 0:1]),
         waits=[('tn', s.cur('tn'))], inc='ve')

    for tc in range(TC):
        co = tc * 128
        for dc in range(DC):
            s.op('tn', (lambda dc=dc, co=co: lambda e: e.matmul(
                b.PS[4][0:8, 0:128], h['RWFM'][:, dc * 8:(dc + 1) * 8],
                X[:, dc * T + co:dc * T + co + 128],
                start=(dc == 0), stop=(dc == DC - 1)))(),
                waits=[('ve', v_rwfm), ('ve', getattr(b, 'v_log_free', 0))],
                inc='tn' if dc == DC - 1 else None)
        tn_log = s.cur('tn')
        t1_8 = b.f32slice(0, 128, rows=8)
        s.op('ve', (lambda co=co, t1_8=t1_8: lambda e: e.tensor_tensor(
            t1_8, b.PS[4][0:8, 0:128],
            b.bfslice(A_RSIG + co, 128, rows=8), ALU.mult))(),
            waits=[('tn', tn_log), ('ve', b.v_stats)], inc='ve')
        logbf = b.bfslice(A_LOG + co, 128, rows=8)
        s.op('ve', (lambda co=co, t1_8=t1_8, logbf=logbf: lambda e: e.scalar_tensor_tensor(
            logbf, b.bfslice(A_NMSH + co, 128, rows=8), h['COLRW'][:],
            t1_8, ALU.mult, ALU.add))(), inc='ve')
        b.v_log_free = s.cur('ve')
        v_log = s.cur('ve')
        slot = 16 + (tc % 2) * 8
        s.op('tn', (lambda logbf=logbf, slot=slot, v_log=v_log: lambda e: e.transpose(
            b.PSB[:, slot:slot + 8], logbf, h['IDENT'][0:8, 0:8]))(),
            waits=[('ve', v_log), ('ve', getattr(b, 'v_top2_done', 0))], inc='tn')
        tn_tr = s.cur('tn')
        lt = b.PSB[:, slot:slot + 8]
        s.op('ve', (lambda lt=lt: lambda e: e.reduce_max(
            h['NEGM1'][:], lt, axis=AX.X, negate=True))(),
            waits=[('tn', tn_tr)], inc='ve')
        s.op('ve', (lambda lt=lt: lambda e: e.tensor_single_scalar(
            h['TM8A'][:], lt, h['NEGM1'][:], ALU.add))(), inc='ve')
        s.op('ve', lambda e: e.tensor_single_scalar(
            h['TM8A'][:], h['TM8A'][:], 0.0, ALU.is_equal), inc='ve')
        s.op('ve', (lambda lt=lt: lambda e: e.scalar_tensor_tensor(
            h['TM8B'][:], h['TM8A'][:], -1e9, lt, ALU.mult, ALU.add))(), inc='ve')
        s.op('ve', lambda e: e.reduce_max(
            h['NEGM2'][:], h['TM8B'][:], axis=AX.X, negate=True), inc='ve')
        v_m2 = s.cur('ve')
        s.op('sc', (lambda lt=lt: lambda e: e.activation(
            h['TM8C'][:], lt, AF.Exp, bias=h['NEGM1'][:]))(),
            waits=[('ve', v_m2)], inc='sc')
        sc_exp = s.cur('sc')
        s.op('ve', (lambda lt=lt: lambda e: e.tensor_single_scalar(
            h['TM8B'][:], lt, h['NEGM2'][:], ALU.add))(), inc='ve')
        s.op('ve', lambda e: e.tensor_single_scalar(
            h['TM8B'][:], h['TM8B'][:], 0.0, ALU.is_ge), inc='ve')
        s.op('ve', (lambda sc_exp=sc_exp: lambda e: e.tensor_tensor(
            h['TM8A'][:], h['TM8C'][:], h['TM8B'][:], ALU.mult))(),
            waits=[('sc', sc_exp)], inc='ve')
        s.op('ve', lambda e: e.reduce_sum(h['RDEN'][:], h['TM8A'][:], axis=AX.X), inc='ve')
        s.op('ve', lambda e: e.reciprocal(h['RDEN'][:], h['RDEN'][:]), inc='ve')
        s.op('ve', lambda e: e.tensor_single_scalar(
            h['WTM'][:], h['TM8A'][:], h['RDEN'][:], ALU.mult), inc='ve')
        v_wtm = s.cur('ve')
        slot2 = 32 + (tc % 2) * 128
        s.op('tn', (lambda slot2=slot2, v_wtm=v_wtm: lambda e: e.transpose(
            b.PSB[:, slot2:slot2 + 128][0:8, :], h['WTM'][:], h['IDENT'][:]))(),
            waits=[('ve', v_wtm)], inc='tn')
        tn_w = s.cur('tn')
        s.op('ve', (lambda co=co, slot2=slot2, tn_w=tn_w: lambda e: e.tensor_copy(
            b.bfslice(A_LOG + co, 128, rows=8), b.PSB[:, slot2:slot2 + 128][0:8, :]))(),
            waits=[('tn', tn_w)], inc='ve')
        b.v_top2_done = s.cur('ve')
    v_wfm = s.cur('ve')
    for qi in range(NQT):
        qo = qi * 512
        s.op('tn', (lambda qo=qo: lambda e: e.matmul(
            b.PS[5][:], h['E2'][:], b.bfslice(A_LOG + qo, 512, rows=8),
            start=True, stop=True))(),
            waits=[('ve', v_wfm), ('ve', getattr(b, 'v_wx2', 0))], inc='tn')
        tn_wx = s.cur('tn')
        s.op('ve', (lambda qo=qo, tn_wx=tn_wx: lambda e: e.tensor_copy(
            b.bfslice(A_WX2 + qo, 512), b.PS[5][:]))(),
            waits=[('tn', tn_wx)], inc='ve')
        b.v_wx2 = s.cur('ve')
    v_wx2 = s.cur('ve')

    afm = lambda dc: b.bfslice(A_AFM + dc * 128, 128)
    for dc in range(DC):
        for qi in range(NQT):
            s.op('tn', (lambda dc=dc, qi=qi: lambda e: e.matmul(
                b.PS[qi][:], afm(dc), X[:, dc * T + qi * 512:dc * T + qi * 512 + 512],
                start=(dc == 0), stop=(dc == DC - 1)))(),
                waits=[afm_toks[dc], ('ve', b.ps_free_stats), ('ve', b.v_log_free)],
                inc=None)
        s.op('tn', (lambda dc=dc: lambda e: e.matmul(
            b.PS[4][0:128, 0:1], afm(dc), h['ONEC'][:],
            start=(dc == 0), stop=(dc == DC - 1)))(),
            inc='tn' if dc == DC - 1 else None)
    tn_mid = s.cur('tn')
    s.op('ve', lambda e: e.tensor_copy(h['COLA'][:], b.PS[4][0:128, 0:1]),
         waits=[('tn', tn_mid)], inc='ve')
    t1 = b.f32slice(0, 512)
    for qi in range(NQT):
        qo = qi * 512
        s.op('ve', (lambda qi=qi, qo=qo, t1=t1: lambda e: e.scalar_tensor_tensor(
            t1[:], mean(qo), h['COLA'][:], b.PS[qi][:], ALU.mult, ALU.subtract))(),
            waits=[('ve', v_wx2)], inc='ve')
        s.op('ve', (lambda qo=qo, t1=t1: lambda e: e.tensor_tensor(
            b.bfslice(A_MIDW + qo, 512), t1[:], b.bfslice(A_WX2 + qo, 512),
            ALU.mult))(), inc='ve')
    v_midw = s.cur('ve')
    b.ps_free_mid = v_midw

    if getattr(b, 'deferred_prep', None):
        b.deferred_prep()
    fuse_prep = getattr(b, 'fuse_bw_prep', False)
    b.fuse_bw_prep = False

    WB = (h['WB0'], h['WB1'])
    wb_free = [list(b.wb_free[0]), list(b.wb_free[1])]
    fin = (b.bfslice(A_FIN, 512), b.bfslice(A_FIN + 512, 512))
    t2 = (b.f32slice(512, 512), b.f32slice(1024, 512))
    g_fin_free = [list(xw), list(xw)]
    last_evac_qh = [0, 0]
    dst_ap = dt[dst]
    prep_cast_free = []
    prep_bank_free = [('ve', s.cur('ve'))]
    prep_store_free = []
    wb_free_prep = [0]
    qh_outer_list = ([0], [1]) if th_outer else (None,)
    for qh_only in qh_outer_list:
     for oc in range(DC):
        if fuse_prep:
            wb = WB[1]
            ct = s.op('gp', (lambda oc=oc: lambda e: e.dma_start(
                out=WB[0][:], in_=dt['bw'][oc * 128:(oc + 1) * 128, :]))(),
                waits=prep_cast_free, inc='dgp')
            copy_toks = []
            for g in range(4):
                for slot in range(8):
                    dcq = g * 8 + slot
                    s.op('tn', (lambda dcq=dcq, slot=slot: lambda e: e.transpose(
                        b.PSB[:, slot * 128:(slot + 1) * 128],
                        WB[0][:, dcq * 128:(dcq + 1) * 128], h['IDENT'][:]))(),
                        waits=[ct] + prep_bank_free, inc='tn' if slot == 7 else None)
                    prep_bank_free = []
                tv = s.cur('tn')
                s.op('ve', (lambda g=g, tv=tv: lambda e: e.tensor_copy(
                    WB[1][:, g * 1024:(g + 1) * 1024], b.PSB[:]))(),
                    waits=[('tn', tv), ('tn', wb_free_prep[0]),
                           ('ve', b.ps_free_mid if oc == 0 else 0)] + prep_store_free,
                    inc='ve')
                prep_bank_free = [('ve', s.cur('ve'))]
            prep_cast_free = [('tn', s.cur('tn'))]
            vcp = s.cur('ve')
            bwc_toks = [('ve', vcp)]
            prep_store_free = [s.op('sy', (lambda oc=oc, vcp=vcp: lambda e: e.dma_start(
                out=dt['s_bwt'][oc * 128:(oc + 1) * 128, :], in_=WB[1][:]))(),
                waits=[('ve', vcp)], inc='dsy')]
        else:
            wb = WB[oc % 2]
            w = wb_free[oc % 2] + list(b.bwt_done)
            bwc_toks = [s.op('sy', (lambda oc=oc, wb=wb: lambda e: e.dma_start(
                out=wb[:], in_=dt['s_bwt'][oc * 128:(oc + 1) * 128, :]))(),
                waits=w, inc='dsy')]
        tn_oc_halves = [0, 0]
        qh_iter = qh_only if qh_only is not None else range(2)
        for qh in qh_iter:
            bk = ((0, 1) if qh == 0 else (2, 3)) if qh_only is None else \
                 ((0, 1) if oc % 2 == 0 else (2, 3))
            for dc in range(DC):
                lhs = wb[:, dc * 128:(dc + 1) * 128]
                for ql in range(2):
                    qi = qh * 2 + ql
                    s.op('tn', (lambda lhs=lhs, dc=dc, qi=qi, bk=bk, ql=ql: lambda e: e.matmul(
                        b.PS[bk[ql]][:], lhs, X[:, dc * T + qi * 512:dc * T + qi * 512 + 512],
                        start=(dc == 0), stop=False))(),
                        waits=bwc_toks + [('ve', b.ps_free_mid if oc == 0 else 0),
                                          ('ve', last_evac_qh[qh if qh_only is None else (oc % 2)]
                                           if dc == 0 else 0)],
                        inc=None)
                if qh == 0:
                    s.op('tn', (lambda lhs=lhs, dc=dc: lambda e: e.matmul(
                        b.PS[4][0:128, 0:1], lhs, h['ONEC'][:],
                        start=(dc == 0), stop=(dc == DC - 1)))(), inc=None)
            for ql in range(2):
                qi = qh * 2 + ql
                s.op('tn', (lambda oc=oc, qi=qi, bk=bk, ql=ql: lambda e: e.matmul(
                    b.PS[bk[ql]][:], b.bfslice(A_BFM + oc * 128, 128),
                    b.bfslice(A_MIDW + qi * 512, 512), start=False, stop=True))(),
                    waits=[g_bfm, ('ve', v_midw)],
                    inc='tn' if ql == 1 else None)
            tn_oc_halves[qh] = s.cur('tn')
            if qh == 0:
                cw = (h['COLWA'][:, oc:oc + 1] if qh_only is not None
                      else h['COLW'][:, oc % 2:oc % 2 + 1])
                s.op('ve', (lambda cw=cw: lambda e: e.tensor_single_scalar(
                    cw, b.PS[4][0:128, 0:1], 0.5, ALU.mult))(),
                     waits=[('tn', tn_oc_halves[qh])], inc='ve')
            elif qh_only is not None:
                cw = h['COLWA'][:, oc:oc + 1]
            for ql in range(2):
                qi = qh * 2 + ql
                qo = qi * 512
                par = (oc * NQT + qi) % 2
                s.op('ve', (lambda qi=qi, qo=qo, t1=t1, bk=bk, ql=ql: lambda e: e.scalar_tensor_tensor(
                    t1[:], b.PS[bk[ql]][:], 0.5, rsig(qo), ALU.mult, ALU.mult))(),
                    waits=[('tn', tn_oc_halves[qh])], inc='ve')
                s.op('ve', (lambda qo=qo, t1=t1, par=par, cw=cw: lambda e: e.scalar_tensor_tensor(
                    t2[par][:], nmsh(qo), cw, t1[:], ALU.mult, ALU.add))(), inc='ve')
                s.op('ve', (lambda oc=oc, qo=qo, par=par: lambda e: e.tensor_tensor(
                    fin[par][:], t2[par][:], X[:, oc * T + qo:oc * T + qo + 512],
                    ALU.add))(),
                    waits=g_fin_free[par], inc='ve')
                v_fin = s.cur('ve')
                tok = s.op('sy', (lambda oc=oc, qo=qo, par=par, v_fin=v_fin, dst_ap=dst_ap:
                            lambda e: e.dma_start(
                    out=dst_ap[oc * 128:(oc + 1) * 128, qo:qo + 512], in_=fin[par][:]))(),
                    waits=[('ve', v_fin)], inc='dsy')
                g_fin_free[par] = [tok]
            last_evac_qh[qh if qh_only is None else (oc % 2)] = s.cur('ve')
        tn_oc_end = tn_oc_halves[qh_iter[-1] if qh_only is not None else 1]
        wb_free[oc % 2] = [('tn', tn_oc_end)]
        wb_free_prep[0] = tn_oc_end
     if th_outer and qh_only == [0]:
        b.fin_stored = s.qall('dsy')
        emit_output(b, dst, tcs=range(0, 8), barrier=False)
    b.fin_stored = s.qall('dsy')
    b.x_consumers = [('ve', s.cur('ve')), ('tn', s.cur('tn'))]
    if fuse_prep:
        b.bwt_done = s.qall('dsy')
        wb_free[0] = prep_cast_free
        wb_free[1] = [('tn', s.cur('tn'))] + prep_store_free
    b.wb_free = wb_free
    b.ps_free_stats = 0
    b.v_log_free = 0
    b.v_top2_done = 0
    b.v_wx2 = 0


def emit_attn_phase(b):
    emit_attn_proj(b)
    emit_attn_core(b)


def emit_attn_proj(b):
    s, h, dt = b.s, b.h, b.dt
    s.phase_barrier()
    X = h['X']
    A_RSIG, A_NMSH, A_MEAN = 0, 2048, 4096
    A_XSQ = 6144
    A_COS = 8192
    A_SIN = 10240
    A_QLN = 12288
    A_QRP = 14336
    rsig = lambda qo, n=512: b.bfslice(A_RSIG + qo, n)
    nmsh = lambda qo, n=512: b.bfslice(A_NMSH + qo, n)

    emit_stats(b, A_RSIG, A_NMSH, A_MEAN, A_XSQ)

    s.op('ve', lambda e: e.tensor_single_scalar(h['PCOL2'][:], h['PCOL'][:], 64.0, ALU.is_ge),
         waits=[('gp', b.g_pcol)], inc='ve')
    s.op('ve', lambda e: e.tensor_single_scalar(h['PCOL2'][:], h['PCOL2'][:], -64.0, ALU.mult),
         inc='ve')
    s.op('ve', lambda e: e.tensor_tensor(h['PCOL2'][:], h['PCOL'][:], h['PCOL2'][:], ALU.add),
         inc='ve')
    v_pm = s.cur('ve')
    s.op('sc', lambda e: e.activation(h['INVF'][:], h['PCOL2'][:], AF.Exp,
                                      scale=-math.log(THETA) / 64.0),
         waits=[('ve', v_pm)], inc='sc')
    sc_invf = s.cur('sc')
    b1 = b.f32slice(0, 512)
    b2 = b.f32slice(512, 512)
    b3 = b.f32slice(1024, 512)
    b4 = b.f32slice(1536, 512)
    ki = h['SCR_I32'][:, 0:512]
    for qq in range(4):
        base = qq * 512
        s.op('gp', (lambda base=base: lambda e: e.iota(
            b1[:], pattern=[[1, 512]], base=base, channel_multiplier=0,
            allow_small_or_imprecise_dtypes=True))(),
            waits=[('ve', getattr(b, 'v_rope_free', 0)),
                   ('ve', b.v_stats), ('sc', b.sc_stats)], inc='gp')
        g_t = s.cur('gp')
        s.op('ve', (lambda g_t=g_t: lambda e: e.tensor_single_scalar(
            b2[:], b1[:], h['INVF'][:], ALU.mult))(),
            waits=[('gp', g_t), ('sc', sc_invf)], inc='ve')
        s.op('ve', lambda e: e.tensor_single_scalar(b3[:], b2[:], INV_2PI, ALU.mult), inc='ve')
        s.op('ve', lambda e: e.tensor_copy(ki[:], b3[:]), inc='ve')
        s.op('ve', lambda e: e.tensor_copy(b1[:], ki[:]), inc='ve')
        s.op('ve', lambda e: e.tensor_tensor(b4[:], b3[:], b1[:], ALU.subtract), inc='ve')
        s.op('ve', lambda e: e.tensor_single_scalar(b4[:], b4[:], 0.5, ALU.is_ge), inc='ve')
        s.op('ve', lambda e: e.tensor_tensor(b1[:], b1[:], b4[:], ALU.add), inc='ve')
        s.op('ve', lambda e: e.tensor_single_scalar(b3[:], b3[:], 0.25, ALU.add), inc='ve')
        s.op('ve', lambda e: e.tensor_copy(ki[:], b3[:]), inc='ve')
        s.op('ve', lambda e: e.tensor_copy(b4[:], ki[:]), inc='ve')
        s.op('ve', lambda e: e.tensor_tensor(b3[:], b3[:], b4[:], ALU.subtract), inc='ve')
        s.op('ve', lambda e: e.tensor_single_scalar(b3[:], b3[:], 0.5, ALU.is_ge), inc='ve')
        s.op('ve', lambda e: e.tensor_tensor(b4[:], b4[:], b3[:], ALU.add), inc='ve')
        s.op('ve', lambda e: e.tensor_single_scalar(b4[:], b4[:], -0.25, ALU.add), inc='ve')
        s.op('ve', lambda e: e.scalar_tensor_tensor(
            b3[:], b1[:], -TWO_PI_HI, b2[:], ALU.mult, ALU.add), inc='ve')
        s.op('ve', lambda e: e.scalar_tensor_tensor(
            b1[:], b1[:], -TWO_PI_LO, b3[:], ALU.mult, ALU.add), inc='ve')
        v_ys = s.cur('ve')
        s.op('sc', (lambda: lambda e: e.activation(b3[:], b1[:], AF.Sin))(),
             waits=[('ve', v_ys)], inc='sc')
        sc_sin = s.cur('sc')
        s.op('ve', (lambda base=base, sc_sin=sc_sin: lambda e: e.tensor_single_scalar(
            b.bfslice(A_SIN + base, 512)[0:64, :], b3[0:64, :], -1.0, ALU.mult))(),
            waits=[('sc', sc_sin)], inc='ve')
        s.op('ve', (lambda base=base: lambda e: e.tensor_single_scalar(
            b.bfslice(A_SIN + base, 512)[64:128, :], b3[64:128, :], 1.0, ALU.mult))(),
            inc='ve')
        s.op('ve', lambda e: e.scalar_tensor_tensor(
            b3[:], b4[:], -TWO_PI_HI, b2[:], ALU.mult, ALU.add), inc='ve')
        s.op('ve', lambda e: e.scalar_tensor_tensor(
            b2[:], b4[:], -TWO_PI_LO, b3[:], ALU.mult, ALU.add), inc='ve')
        v_yc = s.cur('ve')
        s.op('sc', (lambda: lambda e: e.activation(b3[:], b2[:], AF.Sin))(),
             waits=[('ve', v_yc)], inc='sc')
        sc_cos = s.cur('sc')
        s.op('ve', (lambda base=base, sc_cos=sc_cos: lambda e: e.tensor_single_scalar(
            b.bfslice(A_COS + base, 512), b3[:], 1.0, ALU.mult))(),
            waits=[('sc', sc_cos)], inc='ve')
        b.v_rope_free = s.cur('ve')
    v_rope = s.cur('ve')

    WB = (h['WB0'], h['WB1'])
    wb_free = [list(b.wb_free[0]), list(b.wb_free[1])]
    wq_r = dt['wq'].rearrange("(dc p) (hc o) -> hc p dc o", p=128, o=128)
    wk_r = dt['wk'].rearrange("(dc p) (hc o) -> hc p dc o", p=128, o=128)
    wv_r = dt['wv'].rearrange("(dc p) (hc o) -> hc p dc o", p=128, o=128)
    qln = b.bfslice(A_QLN, 2048)
    qrp = b.bfslice(A_QRP, 2048)
    t1 = b.f32slice(0, 512)
    tr = b.f32slice(512, 512)
    v_spill_wait = [[]]

    v_proj_free_qh = [0, 0]

    def proj_head(idx, w_r, scr_dst, do_rope, ih):
        wb = WB[idx % 2]
        w_toks = []
        for piece in range(2):
            w_toks.append(s.op('gp', (lambda wb=wb, piece=piece: lambda e: e.dma_start(
                out=wb.ap().rearrange("p (c j) -> p c j", j=128)[:, piece * 16:(piece + 1) * 16],
                in_=w_r[ih][:, piece * 16:(piece + 1) * 16]))(),
                waits=wb_free[idx % 2] if piece == 0 else [], inc='dgp'))
        colh = h['COLH0'] if idx % 2 == 0 else h['COLH1']
        tn_halves = [0, 0]
        for qh in range(2):
            bk = (0, 1) if qh == 0 else (2, 3)
            for dc in range(DC):
                lhs = wb[:, dc * 128:(dc + 1) * 128]
                for ql in range(2):
                    qi = qh * 2 + ql
                    s.op('tn', (lambda lhs=lhs, dc=dc, qi=qi, bk=bk, ql=ql: lambda e: e.matmul(
                        b.PS[bk[ql]][:], lhs, X[:, dc * T + qi * 512:dc * T + qi * 512 + 512],
                        start=(dc == 0), stop=(dc == DC - 1)))(),
                        waits=w_toks + list(b.x_ready) +
                              [('ve', v_proj_free_qh[qh] if dc == 0 else 0)],
                        inc='tn' if (qh == 1 and dc == DC - 1 and ql == 1) else None)
                if qh == 0:
                    s.op('tn', (lambda lhs=lhs, dc=dc: lambda e: e.matmul(
                        b.PS[4][0:128, 0:1], lhs, h['ONEC'][:],
                        start=(dc == 0), stop=(dc == DC - 1)))(),
                        inc='tn' if dc == DC - 1 else None)
            tn_halves[qh] = s.cur('tn')
            if qh == 0:
                s.op('ve', (lambda colh=colh: lambda e: e.tensor_copy(
                    colh[:], b.PS[4][0:128, 0:1]))(),
                     waits=[('tn', tn_halves[0]), ('ve', b.v_stats), ('ve', v_rope)]
                           + v_spill_wait[0],
                     inc='ve')
            v_lns = {}
            for ql in range(2):
                qi = qh * 2 + ql
                qo = qi * 512
                s.op('ve', (lambda qo=qo, t1=t1, bk=bk, ql=ql: lambda e: e.tensor_tensor(
                    t1[:], b.PS[bk[ql]][:], rsig(qo), ALU.mult))(),
                    waits=[('tn', tn_halves[qh])] + (v_spill_wait[0] if qh == 1 else []),
                    inc='ve')
                s.op('ve', (lambda qo=qo, t1=t1, colh=colh: lambda e: e.scalar_tensor_tensor(
                    qln[:, qo:qo + 512], nmsh(qo), colh[:], t1[:], ALU.mult, ALU.add))(),
                    inc='ve')
                v_lns[ql] = s.cur('ve')
            if do_rope:
                tn_sws = {}
                for ql in range(2):
                    qo = (qh * 2 + ql) * 512
                    s.op('tn', (lambda qo=qo, bk=bk, ql=ql, v_ln=v_lns[ql]:
                                lambda e: e.matmul(
                        b.PS[bk[ql]][:], h['PSW'][:], qln[:, qo:qo + 512],
                        start=True, stop=True))(),
                        waits=[('ve', v_lns[ql])], inc='tn')
                    tn_sws[ql] = s.cur('tn')
                for ql in range(2):
                    qo = (qh * 2 + ql) * 512
                    s.op('ve', (lambda qo=qo, t1=t1: lambda e: e.tensor_tensor(
                        t1[:], qln[:, qo:qo + 512], b.bfslice(A_COS + qo, 512), ALU.mult))(),
                        inc='ve')
                    s.op('ve', (lambda qo=qo, bk=bk, ql=ql:
                                lambda e: e.tensor_tensor(
                        tr[:], b.PS[bk[ql]][:], b.bfslice(A_SIN + qo, 512), ALU.mult))(),
                        waits=[('tn', tn_sws[ql])], inc='ve')
                    s.op('ve', (lambda qo=qo, t1=t1, tr=tr: lambda e: e.tensor_tensor(
                        qrp[:, qo:qo + 512], t1[:], tr[:], ALU.add))(), inc='ve')
            v_proj_free_qh[qh] = s.cur('ve')
        v_done = s.cur('ve')
        src = qrp if do_rope else qln
        tok = s.op('sy', (lambda src=src, scr_dst=scr_dst: lambda e: e.dma_start(
            out=scr_dst, in_=src[:]))(),
            waits=[('ve', v_done)], inc='dsy')
        v_spill_wait[0] = [tok]
        wb_free[idx % 2] = [('tn', tn_halves[1])]

    idx = 0
    for H in range(NKV):
        proj_head(idx, wk_r, dt['s_k'][H * 128:(H + 1) * 128, :], True, H)
        idx += 1
    for H in range(NKV):
        proj_head(idx, wv_r, dt['s_v'][H * 128:(H + 1) * 128, :], False, H)
        idx += 1
    v_spilled = s.qall('dsy')
    vst_free = [[], []]
    for tc in range(TC):
        base_off = 16384 if tc % 2 == 0 else 18432
        xb_toks = []
        for H in range(NKV):
            xb_toks.append(s.op('sc', (lambda H=H, tc=tc, base_off=base_off:
                                       lambda e: e.dma_start_transpose(
                b.bfslice(base_off + H * 128, 128),
                dt['s_v'][H * 128:(H + 1) * 128, tc * 128:(tc + 1) * 128]))(),
                waits=(v_spilled if (tc == 0 and H == 0) else []) + vst_free[tc % 2],
                inc='dsc'))
        st = s.op('sc', (lambda tc=tc, base_off=base_off: lambda e: e.dma_start(
            out=dt['s_vt'][tc * 128:(tc + 1) * 128, :],
            in_=b.bfslice(base_off, 1024)))(),
            waits=xb_toks, inc='dsc')
        vst_free[tc % 2] = [st]
    for hh in range(NH):
        proj_head(idx, wq_r, dt['s_q'][hh * 128:(hh + 1) * 128, :], True, hh)
        idx += 1
    b.spills_done = s.qall('dsy') + s.qall('dsc')
    b.wb_free = wb_free


def emit_attn_core(b):
    s, h, dt = b.s, b.h, b.dt
    s.phase_barrier()
    X = h['X']
    A_CTX = 0
    A_KW = (16384, 17920)
    A_VW = (19456, 20992)
    A_QB = 22528
    A_EX = 23552
    recip = b.f32slice(0, 512)
    recip2 = b.f32slice(512, 512)
    b.v_ctx_evac = [0, 0]
    s.op('ve', lambda e: e.tensor_single_scalar(
        h['MASK1'][:], h['IOTA'][:, 0:128], 0.0, ALU.is_ge), inc='ve')
    s.op('ve', lambda e: e.tensor_single_scalar(
        h['MASK2'][:], h['IOTA'][:, 0:128], -1.0, ALU.is_le), inc='ve')
    v_masks = s.cur('ve')

    wo_r = dt['s_wo'].rearrange("(oc p) (dc j) -> dc p oc j", p=128, j=128)
    vt_r = dt['s_vt'].rearrange("(kc p) (H j) -> H p kc j", p=128, j=128)
    WB = (h['WB0'], h['WB1'])
    wb_free = [list(b.wb_free[0]), list(b.wb_free[1])]
    kv_free = [[], []]
    qb_free = [[], []]
    ex_free = [0, 0, 0]
    ctx_free = [0]
    kv_toks_buf = [[], []]
    b.ex_rot = 0

    def load_kv(qidx, Hkv, par):
        q0 = qidx * 512
        kf = max(0, -(-(q0 - 1151) // 128))
        kl = (q0 + 511) // 128
        nkc = kl - kf + 1
        toks = []
        toks.append(s.op('sy', (lambda Hkv=Hkv, kf=kf, nkc=nkc, par=par: lambda e: e.dma_start(
            out=b.bfslice(A_KW[par], nkc * 128),
            in_=dt['s_k'][Hkv * 128:(Hkv + 1) * 128, kf * 128:kf * 128 + nkc * 128]))(),
            waits=kv_free[par] + list(b.spills_done), inc='dsy'))
        toks.append(s.op('sy', (lambda Hkv=Hkv, kf=kf, nkc=nkc, par=par: lambda e: e.dma_start(
            out=b.bfslice(A_VW[par], nkc * 128).rearrange("p (c j) -> p c j", j=128),
            in_=vt_r[Hkv][:, kf:kf + nkc, :]))(),
            waits=kv_free[par], inc='dsy'))
        kv_toks_buf[par] = toks

    load_kv(0, 0, 0)
    for qi in range(NQT):
        q0 = qi * 512
        kf = max(0, -(-(q0 - 1151) // 128))
        kl = (q0 + 511) // 128
        nkc = kl - kf + 1
        for Hkv in range(NKV):
            par = (qi * NKV + Hkv) % 2
            kv_toks = kv_toks_buf[par]
            nxt = qi * NKV + Hkv + 1
            if nxt < NQT * NKV:
                load_kv(nxt // NKV, nxt % NKV, (par + 1) % 2)
            for g2 in range(0, GQ, 2):
                heads = [Hkv * GQ + g2, Hkv * GQ + g2 + 1]
                g_q = {}
                for sub in (0, 1):
                    hh = heads[sub]
                    g_q[sub] = s.op('sy', (lambda hh=hh, q0=q0, sub=sub: lambda e: e.dma_start(
                        out=b.bfslice(A_QB + sub * 512, 512),
                        in_=dt['s_q'][hh * 128:(hh + 1) * 128, q0:q0 + 512]))(),
                        waits=qb_free[sub] + list(b.spills_done), inc='dsy')
                ps_ctx = (2, 5)
                ps_den = (3, 6)
                exp_done = {0: {}, 1: {}}
                mask_done = {0: {}, 1: {}}
                exslot = {}
                tn_last_den = [0, 0]
                tn_last_score = [0, 0]
                for ci in range(nkc):
                    kc = kf + ci
                    delta = q0 - kc * 128
                    if -384 <= delta <= 0:
                        lo2, hi2 = -delta, 512
                        mlo, mk = lo2, 'MASK1'
                    elif 640 <= delta <= 1024:
                        lo2, hi2 = 0, (1024 - delta) + 128
                        mlo, mk = hi2 - 128, 'MASK2'
                    else:
                        lo2, hi2 = 0, 512
                        mlo, mk = None, None
                    slots = {}
                    for sub in (0, 1):
                        sbank = (0 if ci % 2 == 0 else 4) if sub == 0 else 1
                        sdep = 2 if sub == 0 else 1
                        s.op('tn', (lambda ci=ci, sub=sub, par=par, lo2=lo2, hi2=hi2,
                                    sbank=sbank: lambda e: e.matmul(
                            b.PS[sbank][:, lo2:hi2], b.bfslice(A_KW[par] + ci * 128, 128),
                            b.bfslice(A_QB + sub * 512, 512)[:, lo2:hi2],
                            start=True, stop=True))(),
                            waits=[g_q[sub]] + kv_toks + [('ve', v_masks),
                                   ('sc', exp_done[sub].get(ci - sdep, 0))], inc='tn')
                        tn_s = s.cur('tn')
                        tn_last_score[sub] = tn_s
                        slot = b.ex_rot
                        b.ex_rot = (slot + 1) % 3
                        slots[sub] = slot
                        exb = b.bfslice(A_EX + slot * 512, 512)
                        exslot[(sub, ci)] = exb
                        s.op('sc', (lambda exb=exb, tn_s=tn_s, lo2=lo2, hi2=hi2,
                                    sbank=sbank: lambda e: e.activation(
                            exb[:, lo2:hi2], b.PS[sbank][:, lo2:hi2], AF.Exp,
                            bias=h['CEXP'][:], scale=SM))(),
                            waits=[('tn', tn_s), ('gp', b.g_cexp),
                                   ('tn', ex_free[slot])], inc='sc')
                        exp_done[sub][ci] = s.cur('sc')
                        vm = 0
                        if mk is not None:
                            s.op('gp', (lambda exb=exb, mlo=mlo, mk=mk: lambda e: e.tensor_tensor(
                                exb[:, mlo:mlo + 128], exb[:, mlo:mlo + 128],
                                h[mk][:], ALU.mult))(),
                                waits=[('sc', exp_done[sub][ci])], inc='gp')
                            vm = s.cur('gp')
                        mask_done[sub][ci] = vm
                    for sub in (0, 1):
                        exb = exslot[(sub, ci)]
                        vm = mask_done[sub][ci]
                        plo, phi = lo2, hi2
                        if ci == 0 and (lo2 > 0 or hi2 < 512):
                            if hi2 < 512:
                                s.op('gp', (lambda exb=exb, hi2=hi2: lambda e: e.memset(
                                    exb[:, hi2:512], 0.0))(),
                                    waits=[('sc', exp_done[sub][ci]), ('gp', vm)],
                                    inc='gp')
                            if lo2 > 0:
                                s.op('gp', (lambda exb=exb, lo2=lo2: lambda e: e.memset(
                                    exb[:, 0:lo2], 0.0))(),
                                    waits=[('sc', exp_done[sub][ci]), ('gp', vm)],
                                    inc='gp')
                            vm = s.cur('gp')
                            plo, phi = 0, 512
                        s.op('tn', (lambda ci=ci, exb=exb, first=(ci == 0),
                                    last=(ci == nkc - 1), vm=vm, par=par,
                                    pc=ps_ctx[sub], plo=plo, phi=phi: lambda e: e.matmul(
                            b.PS[pc][:, plo:phi], b.bfslice(A_VW[par] + ci * 128, 128),
                            exb[:, plo:phi], start=first, stop=last))(),
                            waits=[('sc', exp_done[sub][ci]), ('gp', vm),
                                   ('ve', b.v_ctx_evac[sub] if ci == 0 else 0)],
                            inc=None)
                        s.op('tn', (lambda ci=ci, exb=exb, first=(ci == 0),
                                    last=(ci == nkc - 1), pd=ps_den[sub],
                                    plo=plo, phi=phi: lambda e: e.matmul(
                            b.PS[pd][:, plo:phi], h['ONES'][:], exb[:, plo:phi],
                            start=first, stop=last))(),
                            inc='tn')
                        ex_free[slots[sub]] = s.cur('tn')
                        tn_last_den[sub] = s.cur('tn')
                for sub in (0, 1):
                    hh = heads[sub]
                    qb_free[sub] = [('tn', tn_last_score[sub])]
                    rc = recip if sub == 0 else recip2
                    s.op('ve', (lambda rc=rc, pd=ps_den[sub]:
                                lambda e: e.reciprocal(rc[:], b.PS[pd][:]))(),
                        waits=[('tn', tn_last_den[sub])], inc='ve')
                    s.op('ve', (lambda hh=hh, rc=rc, pc=ps_ctx[sub]: lambda e: e.tensor_tensor(
                        b.bfslice(A_CTX + hh * 512, 512), b.PS[pc][:], rc[:], ALU.mult))(),
                        waits=[('tn', ctx_free[0])], inc='ve')
                    b.v_ctx_evac[sub] = s.cur('ve')
            kv_free[par] = [('tn', s.cur('tn'))]
        v_ctx_ready = s.cur('ve')
        last_x_evac = 0
        for dc in range(DC):
            wb = WB[dc % 2]
            wo_toks = []
            for piece in range(2):
                wo_toks.append(s.op('sy', (lambda dc=dc, piece=piece, wb=wb: lambda e: e.dma_start(
                    out=wb.ap().rearrange("p (c j) -> p c j", j=128)[:, piece * 16:(piece + 1) * 16],
                    in_=wo_r[dc][:, piece * 16:(piece + 1) * 16]))(),
                    waits=(wb_free[dc % 2] + list(b.wo_prep)) if piece == 0 else [], inc='dsy'))
            pso = 2 if dc % 2 == 0 else 5
            for ocp in range(DC):
                s.op('tn', (lambda ocp=ocp, wb=wb, pso=pso: lambda e: e.matmul(
                    b.PS[pso][:], wb[:, ocp * 128:(ocp + 1) * 128],
                    b.bfslice(A_CTX + ocp * 512, 512),
                    start=(ocp == 0), stop=(ocp == DC - 1)))(),
                    waits=wo_toks + [('ve', v_ctx_ready),
                                     ('ve', last_x_evac if ocp == 0 else 0)],
                    inc='tn' if ocp == DC - 1 else None)
            tn_o = s.cur('tn')
            wb_free[dc % 2] = [('tn', tn_o)]
            s.op('ve', (lambda dc=dc, q0=q0, pso=pso, tn_o=tn_o: lambda e: e.tensor_tensor(
                X[:, dc * T + q0:dc * T + q0 + 512], b.PS[pso][:],
                X[:, dc * T + q0:dc * T + q0 + 512], ALU.add))(),
                waits=[('tn', tn_o)], inc='ve')
            if dc % 2 == 1 or dc == DC - 1:
                last_x_evac = s.cur('ve')
        ctx_free[0] = s.cur('tn')
        b.v_ctx_evac = [s.cur('ve'), s.cur('ve')]
    b.x_ready = []
    b.wb_free = wb_free
    b.fin_stored = []


def emit_output(b, src, tcs=None, barrier=True):
    s, h, dt = b.s, b.h, b.dt
    if tcs is None:
        tcs = range(TC)
    OSTG = (20480, 6144)
    ost_free = getattr(b, 'ost_free', {20480: [], 6144: []})
    for tc in tcs:
        for half in range(2):
            base = OSTG[half]
            g_tr = s.op('sc', (lambda tc=tc, src=src, base=base, half=half:
                               lambda e: e.dma_start_transpose(
                b.bfslice(base, 2048),
                dt[src][half * 2048:(half + 1) * 2048, tc * 128:(tc + 1) * 128]))(),
                waits=ost_free[base] + list(b.fin_stored), inc='dsc')
            g_out = s.op('gp', (lambda tc=tc, base=base, half=half: lambda e: e.dma_start(
                out=dt['out'][tc * 128:(tc + 1) * 128, half * 2048:(half + 1) * 2048],
                in_=b.bfslice(base, 2048)))(),
                waits=[g_tr], inc='dgp')
            ost_free[base] = [g_out]
    b.ost_free = ost_free
    if barrier:
        s.barrier_wait('gp', s.qall('dgp'))


_NC_CACHE = {}


def _get_nc(phases):
    key = tuple(phases)
    if key not in _NC_CACHE:
        _NC_CACHE[key] = build(phases)
    return _NC_CACHE[key]


def kernel(L0_attn_x, L0_attn_wq, L0_attn_wk, L0_attn_wv, L0_attn_wo,
           L0_ffn_base_W, L0_ffn_router_W, L0_ffn_lora_A, L0_ffn_lora_B):
    nc = _get_nc(PHASES)
    base = {
        "wq": np.ascontiguousarray(L0_attn_wq),
        "wk": np.ascontiguousarray(L0_attn_wk),
        "wv": np.ascontiguousarray(L0_attn_wv),
        "wo": np.ascontiguousarray(L0_attn_wo),
        "bw": np.ascontiguousarray(L0_ffn_base_W),
        "rw": np.ascontiguousarray(L0_ffn_router_W),
        "la": np.ascontiguousarray(L0_ffn_lora_A),
        "lb": np.ascontiguousarray(L0_ffn_lora_B),
    }
    in_maps = []
    for bb in range(8):
        m = dict(base)
        m["x"] = np.ascontiguousarray(L0_attn_x[bb])
        in_maps.append(m)
    res = run_bass_kernel_spmd(nc, in_maps, core_ids=list(range(8)))
    return np.stack([np.asarray(r["out"]) for r in res.results], axis=0)


# version 19
# speedup vs baseline: 1.0549x; 1.0549x over previous
import math
import contextlib
import numpy as np
import ml_dtypes
import concourse.bass as bass
import concourse.mybir as mybir
from concourse.bass_utils import run_bass_kernel_spmd

F32 = mybir.dt.float32
BF16 = mybir.dt.bfloat16
I32 = mybir.dt.int32
AF = mybir.ActivationFunctionType
ALU = mybir.AluOpType
AX = mybir.AxisListType

D = 4096
T = 2048
DC = 32
TC = 16
NQT = 4
QW = 512
NH = 32
NKV = 8
GQ = 4
DH = 128
E = 8
R = 16
WIN = 1024
SM = DH ** -0.5
EXPC = 12.0
EPS = 1e-5
THETA = 10000.0
TWO_PI_HI = float(np.float32(2.0 * math.pi))
TWO_PI_LO = float(np.float32(2.0 * math.pi - np.float32(2.0 * math.pi)))
INV_2PI = float(np.float32(1.0 / (2.0 * math.pi)))

PHASES = (1, 2, 3)

ENGS = ('tn', 've', 'sc', 'gp', 'sy')
ENG_SEMS = ('tn', 've', 'sc', 'gp')
QUEUES = {'dgp': 6, 'dsy': 6, 'dsc': 4}
LANE_SEMS = tuple(f"{q}{i}" for q, n in QUEUES.items() for i in range(n))
ALL_SEMS = ENG_SEMS + LANE_SEMS


class Sch:
    def __init__(self):
        self.ops = {e: [] for e in ENGS}
        self.cnt = {s: 0 for s in ALL_SEMS}
        self.hwm = {e: {} for e in ENGS}
        self.rot = {q: 0 for q in QUEUES}

    def cur(self, sem):
        return self.cnt[sem]

    def op(self, eng, fn, waits=(), inc=None):
        wl = list(waits)
        if inc in QUEUES:
            lane_i = self.rot[inc]
            self.rot[inc] = (lane_i + 1) % QUEUES[inc]
            lane = f"{inc}{lane_i}"
            wl.append((lane, self.cnt[lane]))
            inc = lane
            amt = 16
        else:
            amt = 1
        fl = []
        h = self.hwm[eng]
        for (s, v) in wl:
            if v <= 0:
                continue
            if s == eng:
                continue
            if h.get(s, 0) >= v:
                continue
            fl.append((s, v))
            h[s] = v
        if inc is not None:
            self.cnt[inc] += amt
        self.ops[eng].append((fn, tuple(fl), inc))
        if inc is not None:
            return (inc, self.cnt[inc])
        return None

    def qall(self, q):
        return [(f"{q}{i}", self.cnt[f"{q}{i}"]) for i in range(QUEUES[q])
                if self.cnt[f"{q}{i}"] > 0]

    def barrier_wait(self, eng, waits):
        self.op(eng, None, waits=waits, inc=None)

    def snapshot(self):
        return [(k, self.cnt[k]) for k in ALL_SEMS if self.cnt[k] > 0]

    def phase_barrier(self):
        snap = self.snapshot()
        for e in ENGS:
            self.barrier_wait(e, snap)


def _emit(block, sch, sems):
    def mk(name):
        autodrain = name in ('ve', 'gp')

        def f(e):
            for (fn, waits, inc) in sch.ops[name]:
                for (s, v) in waits:
                    e.wait_ge(sems[s], v)
                if fn is None:
                    continue
                if autodrain:
                    e.drain()
                inst = fn(e)
                if inc is not None:
                    amt = 16 if inc in LANE_SEMS else 1
                    inst.then_inc(sems[inc], amt)
        return f
    block.tensor(mk('tn'))
    block.vector(mk('ve'))
    block.scalar(mk('sc'))
    block.gpsimd(mk('gp'))
    block.sync(mk('sy'))


class Builder:
    def __init__(self, nc, sch, hd):
        self.nc = nc
        self.s = sch
        self.h = hd

    def bfslice(self, off, cols, rows=128):
        return self.h['SCR_BF'][0:rows, off:off + cols]

    def f32slice(self, off, cols, rows=128):
        return self.h['SCR_F32'][0:rows, off:off + cols]


def build(phases=PHASES):
    nc = bass.Bass("TRN2", target_bir_lowering=False)
    dt = {}
    dt['x'] = nc.dram_tensor("x", [T, D], F32, kind="ExternalInput").ap()
    dt['wq'] = nc.dram_tensor("wq", [D, D], F32, kind="ExternalInput").ap()
    dt['wk'] = nc.dram_tensor("wk", [D, NKV * DH], F32, kind="ExternalInput").ap()
    dt['wv'] = nc.dram_tensor("wv", [D, NKV * DH], F32, kind="ExternalInput").ap()
    dt['wo'] = nc.dram_tensor("wo", [D, D], F32, kind="ExternalInput").ap()
    dt['bw'] = nc.dram_tensor("bw", [D, D], F32, kind="ExternalInput").ap()
    dt['rw'] = nc.dram_tensor("rw", [E, D], F32, kind="ExternalInput").ap()
    dt['la'] = nc.dram_tensor("la", [E, D, R], F32, kind="ExternalInput").ap()
    dt['lb'] = nc.dram_tensor("lb", [E, R, D], F32, kind="ExternalInput").ap()
    dt['out'] = nc.dram_tensor("out", [T, D], F32, kind="ExternalOutput").ap()
    dt['s_x0'] = nc.dram_tensor("s_x0", [T, D], BF16, kind="Internal").ap()
    dt['s_bwt'] = nc.dram_tensor("s_bwt", [D, D], BF16, kind="Internal").ap()
    dt['s_x1'] = nc.dram_tensor("s_x1", [D, T], BF16, kind="Internal").ap()
    dt['s_fin'] = nc.dram_tensor("s_fin", [D, T], BF16, kind="Internal").ap()
    dt['s_q'] = nc.dram_tensor("s_q", [NH * DH, T], BF16, kind="Internal").ap()
    dt['s_k'] = nc.dram_tensor("s_k", [NKV * DH, T], BF16, kind="Internal").ap()
    dt['s_v'] = nc.dram_tensor("s_v", [NKV * DH, T], BF16, kind="Internal").ap()
    dt['s_vt'] = nc.dram_tensor("s_vt", [T, NKV * DH], BF16, kind="Internal").ap()
    dt['s_wo'] = nc.dram_tensor("s_wo", [D, D], BF16, kind="Internal").ap()

    with contextlib.ExitStack() as ctx:
        En = ctx.enter_context
        h = {}
        h['X'] = En(nc.sbuf_tensor("X", [128, DC * T], BF16))
        h['WB0'] = En(nc.sbuf_tensor("WB0", [128, 4096], BF16))
        h['WB1'] = En(nc.sbuf_tensor("WB1", [128, 4096], BF16))
        h['IOTA'] = En(nc.sbuf_tensor("IOTA", [128, 128], F32))
        h['IDENT'] = En(nc.sbuf_tensor("IDENT", [128, 128], BF16))
        h['ONES'] = En(nc.sbuf_tensor("ONESM", [128, 128], BF16))
        h['PSW'] = En(nc.sbuf_tensor("PSW", [128, 128], BF16))
        h['ONEC'] = En(nc.sbuf_tensor("ONEC", [128, 1], BF16))
        h['E2'] = En(nc.sbuf_tensor("E2", [8, 128], BF16))
        h['E2F'] = En(nc.sbuf_tensor("E2F", [8, 128], F32))
        h['RWFM'] = En(nc.sbuf_tensor("RWFM", [128, DC * E], BF16))
        h['COLW'] = En(nc.sbuf_tensor("COLW", [128, 2], F32))
        h['COLA'] = En(nc.sbuf_tensor("COLA", [128, 1], F32))
        h['COLH0'] = En(nc.sbuf_tensor("COLH0", [128, 1], F32))
        h['COLH1'] = En(nc.sbuf_tensor("COLH1", [128, 1], F32))
        h['COLRW'] = En(nc.sbuf_tensor("COLRW", [8, 1], F32))
        h['NEGM1'] = En(nc.sbuf_tensor("NEGM1", [128, 1], F32))
        h['NEGM2'] = En(nc.sbuf_tensor("NEGM2", [128, 1], F32))
        h['RDEN'] = En(nc.sbuf_tensor("RDEN", [128, 1], F32))
        h['TM8A'] = En(nc.sbuf_tensor("TM8A", [128, 8], F32))
        h['TM8B'] = En(nc.sbuf_tensor("TM8B", [128, 8], F32))
        h['TM8C'] = En(nc.sbuf_tensor("TM8C", [128, 8], F32))
        h['WTM'] = En(nc.sbuf_tensor("WTM", [128, 8], BF16))
        h['PCOL'] = En(nc.sbuf_tensor("PCOL", [128, 1], F32))
        h['PCOL2'] = En(nc.sbuf_tensor("PCOL2", [128, 1], F32))
        h['INVF'] = En(nc.sbuf_tensor("INVF", [128, 1], F32))
        h['CEXP'] = En(nc.sbuf_tensor("CEXP", [128, 1], F32))
        h['MASK1'] = En(nc.sbuf_tensor("MASK1", [128, 128], BF16))
        h['MASK2'] = En(nc.sbuf_tensor("MASK2", [128, 128], BF16))
        h['SCR_BF'] = En(nc.sbuf_tensor("SCR_BF", [128, 25088], BF16))
        h['SCR_F32'] = En(nc.sbuf_tensor("SCR_F32", [128, 2048], F32))
        h['SCR_I32'] = En(nc.sbuf_tensor("SCR_I32", [128, 512], I32))
        PS = [En(nc.psum_tensor(f"ps{i}", [128, 512], F32)) for i in range(7)]
        PSB = En(nc.psum_tensor("psb", [128, 1024], BF16))
        sems = {s: En(nc.semaphore(s)) for s in ALL_SEMS}
        block = En(nc.Block())

        sch = Sch()
        b = Builder(nc, sch, h)
        b.PS = PS
        b.PSB = PSB
        b.dt = dt
        emit_program(b, phases)
        _emit(block, sch, sems)
    return nc


def emit_program(b, phases):
    emit_startup(b)
    if 1 in phases:
        emit_ffn_phase(b, dst='s_x1', barrier=False)
        emit_x_reload(b, 's_x1')
    if 2 in phases:
        emit_attn_phase(b)
    if 3 in phases:
        emit_ffn_phase(b, dst='s_fin')
        emit_output(b, 's_fin')
    elif 1 in phases and 2 not in phases:
        emit_output(b, 's_x1')
    elif 2 in phases:
        emit_x_dump(b, 's_fin')
        emit_output(b, 's_fin')


def emit_startup(b):
    s, h, dt = b.s, b.h, b.dt
    X, WB = h['X'], (h['WB0'], h['WB1'])

    s.op('gp', lambda e: e.iota(h['IOTA'][:], pattern=[[1, 128]], base=0,
                                channel_multiplier=-1,
                                allow_small_or_imprecise_dtypes=True), inc='gp')
    g_iota = s.cur('gp')
    s.op('gp', lambda e: e.memset(h['ONES'][:], 1.0), inc='gp')
    s.op('gp', lambda e: e.memset(h['ONEC'][:], 1.0), inc='gp')
    s.op('gp', lambda e: e.memset(h['CEXP'][:], -EXPC), inc='gp')
    b.g_cexp = s.cur('gp')
    s.op('gp', lambda e: e.iota(h['E2F'][:], pattern=[[1, 8], [0, 16]], base=0,
                                channel_multiplier=-1,
                                allow_small_or_imprecise_dtypes=True), inc='gp')
    g_e2 = s.cur('gp')
    s.op('gp', lambda e: e.iota(h['PCOL'][:], pattern=[[0, 1]], base=0,
                                channel_multiplier=1,
                                allow_small_or_imprecise_dtypes=True), inc='gp')
    b.g_pcol = s.cur('gp')
    s.op('ve', lambda e: e.tensor_single_scalar(h['IDENT'][:], h['IOTA'][:, 0:128],
                                                0.0, ALU.is_equal),
         waits=[('gp', g_iota)], inc='ve')
    s.op('ve', lambda e: e.tensor_single_scalar(h['PSW'][0:64, :], h['IOTA'][0:64, 0:128],
                                                64.0, ALU.is_equal), inc='ve')
    s.op('ve', lambda e: e.tensor_single_scalar(h['PSW'][64:128, :], h['IOTA'][64:128, 0:128],
                                                -64.0, ALU.is_equal), inc='ve')
    s.op('ve', lambda e: e.tensor_single_scalar(h['E2F'][:], h['E2F'][:],
                                                0.0, ALU.is_equal),
         waits=[('gp', g_e2)], inc='ve')
    s.op('ve', lambda e: e.tensor_single_scalar(h['E2'][:], h['E2F'][:],
                                                -2.0, ALU.mult), inc='ve')
    b.v_const = s.cur('ve')

    XSTG = (20480, 8192)
    xst_free = [[], []]
    psb_bank_free = [0]
    X_r = X.ap().rearrange("p (dc t) -> p dc t", t=T)
    for tc in range(TC):
        base = XSTG[tc % 2]
        tok = s.op('gp', (lambda tc=tc, base=base: lambda e: e.dma_start(
            out=b.bfslice(base, 4096), in_=dt['x'][tc * 128:(tc + 1) * 128, :]))(),
            waits=xst_free[tc % 2], inc='dgp')
        for g in range(4):
            for slot in range(8):
                dc = g * 8 + slot
                s.op('tn', (lambda dc=dc, slot=slot, base=base: lambda e: e.transpose(
                    b.PSB[:, slot * 128:(slot + 1) * 128],
                    b.bfslice(base + dc * 128, 128),
                    h['IDENT'][:]))(),
                    waits=[tok, ('ve', b.v_const), ('ve', psb_bank_free[0])],
                    inc='tn' if slot == 7 else None)
            tv = s.cur('tn')
            s.op('ve', (lambda tc=tc, g=g, tv=tv, X_r=X_r: lambda e: e.tensor_copy(
                X_r[:, g * 8:(g + 1) * 8, tc * 128:(tc + 1) * 128],
                b.PSB.ap().rearrange("p (slot j) -> p slot j", j=128)))(),
                waits=[('tn', tv)], inc='ve')
            psb_bank_free[0] = s.cur('ve')
        xst_free[tc % 2] = [('tn', s.cur('tn'))]
    b.x_ready = [('ve', s.cur('ve'))]
    b.psb_free = ('ve', s.cur('ve'))
    b.xstage_free = xst_free[0] + xst_free[1]

    def _prep():
        for oc in range(DC):
            s.op('gp', (lambda oc=oc: lambda e: e.dma_start(
                out=dt['s_wo'][oc * 128:(oc + 1) * 128, :],
                in_=dt['wo'][oc * 128:(oc + 1) * 128, :]))(), inc='dgp')
        b.wo_prep = s.qall('dgp')
        b.deferred_prep = None
        b.fuse_bw_prep = True
    b.deferred_prep = _prep
    b.bwstage_free = []
    b.wb_free = [[], []]
    b.bwt_done = []
    b.wo_prep = []


def emit_x_reload(b, src):
    s, h, dt = b.s, b.h, b.dt
    waits = list(getattr(b, 'x_consumers', []))
    fin_by_oc = getattr(b, 'fin_toks_by_oc', None)
    toks = []
    for dc in range(DC):
        w = list(waits)
        if fin_by_oc is not None:
            w += fin_by_oc[dc]
        else:
            w += list(b.fin_stored)
        toks.append(s.op('sy', (lambda dc=dc: lambda e: e.dma_start(
            out=h['X'][:, dc * T:(dc + 1) * T],
            in_=dt[src][dc * 128:(dc + 1) * 128, :]))(),
            waits=w, inc='dsy'))
    b.x_ready = toks
    b.x_ready_by_dc = toks
    b.x_consumers = []


def emit_x_dump(b, dst):
    s, h, dt = b.s, b.h, b.dt
    s.phase_barrier()
    toks = []
    for dc in range(DC):
        toks.append(s.op('sy', (lambda dc=dc: lambda e: e.dma_start(
            out=dt[dst][dc * 128:(dc + 1) * 128, :],
            in_=h['X'][:, dc * T:(dc + 1) * T]))(), inc='dsy'))
    b.fin_stored = toks


def emit_stats(b, rsig_o, nmsh_o, mean_o, xsq_off):
    s, h = b.s, b.h
    X = b.h['X']
    stage_w = list(getattr(b, 'bwstage_free', []))
    b.bwstage_free = []
    t1 = b.f32slice(0, 512)
    t2 = b.f32slice(512, 512)
    xsq = (b.bfslice(xsq_off, 1024), b.bfslice(xsq_off + 1024, 1024))
    tn_xsq_read = [0, 0]
    for half in range(2):
        co = half * 1024
        sc_sq = [0, 0]
        xr_by_dc = getattr(b, 'x_ready_by_dc', None)
        for dc in range(DC):
            xb = xsq[dc % 2]
            xw_dc = [xr_by_dc[dc]] if xr_by_dc is not None else list(b.x_ready)
            w = xw_dc + [('tn', tn_xsq_read[dc % 2])] + stage_w
            s.op('sc', (lambda xb=xb, dc=dc, co=co: lambda e: e.activation(
                xb[:], X[:, dc * T + co:dc * T + co + 1024], AF.Square))(),
                waits=w, inc='sc')
            sc_sq[dc % 2] = s.cur('sc')
            for qi in range(2):
                qo = co + qi * 512
                s.op('tn', (lambda dc=dc, qi=qi, qo=qo: lambda e: e.matmul(
                    b.PS[qi][:], h['ONES'][:], X[:, dc * T + qo:dc * T + qo + 512],
                    start=(dc == 0), stop=(dc == DC - 1)))(),
                    waits=[('ve', b.v_const), ('ve', getattr(b, 'ps_free_stats', 0))]
                          + xw_dc,
                    inc=None)
                s.op('tn', (lambda xb=xb, qi=qi, co2=qi * 512, dc=dc:
                            lambda e: e.matmul(
                    b.PS[2 + qi][:], h['ONES'][:], xb[:, co2:co2 + 512],
                    start=(dc == 0), stop=(dc == DC - 1)))(),
                    waits=[('sc', sc_sq[dc % 2])],
                    inc='tn' if qi == 1 else None)
            tn_xsq_read[dc % 2] = s.cur('tn')
        tn_stats = s.cur('tn')
        for qi in range(2):
            qo = co + qi * 512
            mean_sl = b.bfslice(mean_o + qo, 512)
            s.op('ve', (lambda qi=qi, mean_sl=mean_sl: lambda e: e.tensor_single_scalar(
                mean_sl, b.PS[qi][:], 1.0 / D, ALU.mult))(),
                waits=[('tn', tn_stats)] + stage_w, inc='ve')
            s.op('ve', (lambda mean_sl=mean_sl: lambda e: e.tensor_tensor(
                t1[:], mean_sl, mean_sl, ALU.mult))(), inc='ve')
            s.op('ve', (lambda qi=qi: lambda e: e.scalar_tensor_tensor(
                t2[:], b.PS[2 + qi][:], 1.0 / D, t1[:], ALU.mult, ALU.subtract))(),
                inc='ve')
            s.op('ve', (lambda: lambda e: e.tensor_single_scalar(
                t2[:], t2[:], EPS, ALU.add))(), inc='ve')
            v_var = s.cur('ve')
            s.op('sc', (lambda: lambda e: e.activation(t1[:], t2[:], AF.Sqrt))(),
                waits=[('ve', v_var)], inc='sc')
            sc_sqrt = s.cur('sc')
            s.op('ve', (lambda: lambda e: e.reciprocal(t2[:], t1[:]))(),
                waits=[('sc', sc_sqrt)], inc='ve')
            s.op('ve', (lambda qo=qo: lambda e: e.tensor_single_scalar(
                b.bfslice(rsig_o + qo, 512), t2[:], 1.0, ALU.mult))(), inc='ve')
            s.op('ve', (lambda qo=qo, mean_sl=mean_sl: lambda e: e.scalar_tensor_tensor(
                b.bfslice(nmsh_o + qo, 512), mean_sl, -1.0, t2[:], ALU.mult, ALU.mult))(),
                inc='ve')
        b.ps_free_stats = s.cur('ve')
    b.v_stats = s.cur('ve')
    b.sc_stats = s.cur('sc')


def emit_ffn_phase(b, dst, barrier=True):
    s, h, dt = b.s, b.h, b.dt
    if barrier:
        s.phase_barrier()
    X = h['X']
    A_RSIG, A_NMSH, A_MEAN = 0, 2048, 4096
    A_XSQ = 6144
    A_AFM = 8192
    A_BFM = 12288
    A_MIDW = 16384
    A_WX2 = 18432
    A_LOG = 20480
    A_FIN = 22528
    rsig = lambda qo, n=512: b.bfslice(A_RSIG + qo, n)
    nmsh = lambda qo, n=512: b.bfslice(A_NMSH + qo, n)
    mean = lambda qo, n=512: b.bfslice(A_MEAN + qo, n)

    RWN = b.bfslice(A_MIDW, 4096, rows=8)
    g_rwn = s.op('gp', (lambda RWN=RWN: lambda e: e.dma_start(
        out=RWN, in_=dt['rw'][:]))(), inc='dgp')
    xw = list(getattr(b, 'xstage_free', []))
    b.xstage_free = []
    g_bfm = s.op('gp', lambda e: e.dma_start(
        out=b.bfslice(A_BFM, 4096), in_=dt['lb'].rearrange("e r d -> (e r) d")),
        waits=xw, inc='dgp')
    la_r = dt['la'].rearrange("e (dc p) r -> dc p e r", p=128)
    afm_toks = []
    for dc in range(DC):
        afm_toks.append(s.op('gp', (lambda dc=dc: lambda e: e.dma_start(
            out=h['SCR_BF'].ap().rearrange("p (c e r) -> p c e r", e=8, r=16)[:, (A_AFM // 128) + dc],
            in_=la_r[dc]))(), waits=xw, inc='dgp'))
    emit_stats(b, A_RSIG, A_NMSH, A_MEAN, A_XSQ)

    for dc in range(DC):
        s.op('tn', (lambda dc=dc: lambda e: e.transpose(
            b.PSB[:, dc % 2 * 8: dc % 2 * 8 + 8],
            b.bfslice(A_MIDW + dc * 128, 128, rows=8),
            h['IDENT'][0:8, 0:8]))(),
            waits=[g_rwn, ('ve', b.v_const), ('ve', s.cur('ve') if dc >= 2 else 0),
                   getattr(b, 'psb_free', ('ve', 0))],
            inc='tn')
        tv = s.cur('tn')
        s.op('ve', (lambda dc=dc, tv=tv: lambda e: e.tensor_copy(
            h['RWFM'][:, dc * 8:(dc + 1) * 8], b.PSB[:, dc % 2 * 8: dc % 2 * 8 + 8]))(),
            waits=[('tn', tv)], inc='ve')
    v_rwfm = s.cur('ve')
    for dc in range(DC):
        s.op('tn', (lambda dc=dc: lambda e: e.matmul(
            b.PS[5][0:8, 0:1], h['RWFM'][:, dc * 8:(dc + 1) * 8], h['ONEC'][:],
            start=(dc == 0), stop=(dc == DC - 1)))(),
            waits=[('ve', v_rwfm)], inc='tn' if dc == DC - 1 else None)
    s.op('ve', lambda e: e.tensor_copy(h['COLRW'][:], b.PS[5][0:8, 0:1]),
         waits=[('tn', s.cur('tn'))], inc='ve')

    for tc in range(TC):
        co = tc * 128
        for dc in range(DC):
            s.op('tn', (lambda dc=dc, co=co: lambda e: e.matmul(
                b.PS[4][0:8, 0:128], h['RWFM'][:, dc * 8:(dc + 1) * 8],
                X[:, dc * T + co:dc * T + co + 128],
                start=(dc == 0), stop=(dc == DC - 1)))(),
                waits=[('ve', v_rwfm), ('ve', getattr(b, 'v_log_free', 0))],
                inc='tn' if dc == DC - 1 else None)
        tn_log = s.cur('tn')
        t1_8 = b.f32slice(0, 128, rows=8)
        s.op('ve', (lambda co=co, t1_8=t1_8: lambda e: e.tensor_tensor(
            t1_8, b.PS[4][0:8, 0:128],
            b.bfslice(A_RSIG + co, 128, rows=8), ALU.mult))(),
            waits=[('tn', tn_log), ('ve', b.v_stats)], inc='ve')
        logbf = b.bfslice(A_LOG + co, 128, rows=8)
        s.op('ve', (lambda co=co, t1_8=t1_8, logbf=logbf: lambda e: e.scalar_tensor_tensor(
            logbf, b.bfslice(A_NMSH + co, 128, rows=8), h['COLRW'][:],
            t1_8, ALU.mult, ALU.add))(), inc='ve')
        b.v_log_free = s.cur('ve')
        v_log = s.cur('ve')
        slot = 16 + (tc % 2) * 8
        s.op('tn', (lambda logbf=logbf, slot=slot, v_log=v_log: lambda e: e.transpose(
            b.PSB[:, slot:slot + 8], logbf, h['IDENT'][0:8, 0:8]))(),
            waits=[('ve', v_log), ('ve', getattr(b, 'v_top2_done', 0))], inc='tn')
        tn_tr = s.cur('tn')
        lt = b.PSB[:, slot:slot + 8]
        s.op('ve', (lambda lt=lt: lambda e: e.reduce_max(
            h['NEGM1'][:], lt, axis=AX.X, negate=True))(),
            waits=[('tn', tn_tr)], inc='ve')
        s.op('ve', (lambda lt=lt: lambda e: e.tensor_single_scalar(
            h['TM8A'][:], lt, h['NEGM1'][:], ALU.add))(), inc='ve')
        s.op('ve', lambda e: e.tensor_single_scalar(
            h['TM8A'][:], h['TM8A'][:], 0.0, ALU.is_equal), inc='ve')
        s.op('ve', (lambda lt=lt: lambda e: e.scalar_tensor_tensor(
            h['TM8B'][:], h['TM8A'][:], -1e9, lt, ALU.mult, ALU.add))(), inc='ve')
        s.op('ve', lambda e: e.reduce_max(
            h['NEGM2'][:], h['TM8B'][:], axis=AX.X, negate=True), inc='ve')
        v_m2 = s.cur('ve')
        s.op('sc', (lambda lt=lt: lambda e: e.activation(
            h['TM8C'][:], lt, AF.Exp, bias=h['NEGM1'][:]))(),
            waits=[('ve', v_m2)], inc='sc')
        sc_exp = s.cur('sc')
        s.op('ve', (lambda lt=lt: lambda e: e.tensor_single_scalar(
            h['TM8B'][:], lt, h['NEGM2'][:], ALU.add))(), inc='ve')
        s.op('ve', lambda e: e.tensor_single_scalar(
            h['TM8B'][:], h['TM8B'][:], 0.0, ALU.is_ge), inc='ve')
        s.op('ve', (lambda sc_exp=sc_exp: lambda e: e.tensor_tensor(
            h['TM8A'][:], h['TM8C'][:], h['TM8B'][:], ALU.mult))(),
            waits=[('sc', sc_exp)], inc='ve')
        s.op('ve', lambda e: e.reduce_sum(h['RDEN'][:], h['TM8A'][:], axis=AX.X), inc='ve')
        s.op('ve', lambda e: e.reciprocal(h['RDEN'][:], h['RDEN'][:]), inc='ve')
        s.op('ve', lambda e: e.tensor_single_scalar(
            h['WTM'][:], h['TM8A'][:], h['RDEN'][:], ALU.mult), inc='ve')
        v_wtm = s.cur('ve')
        slot2 = 32 + (tc % 2) * 128
        s.op('tn', (lambda slot2=slot2, v_wtm=v_wtm: lambda e: e.transpose(
            b.PSB[:, slot2:slot2 + 128][0:8, :], h['WTM'][:], h['IDENT'][:]))(),
            waits=[('ve', v_wtm)], inc='tn')
        tn_w = s.cur('tn')
        s.op('ve', (lambda co=co, slot2=slot2, tn_w=tn_w: lambda e: e.tensor_copy(
            b.bfslice(A_LOG + co, 128, rows=8), b.PSB[:, slot2:slot2 + 128][0:8, :]))(),
            waits=[('tn', tn_w)], inc='ve')
        b.v_top2_done = s.cur('ve')
    v_wfm = s.cur('ve')
    for qi in range(NQT):
        qo = qi * 512
        s.op('tn', (lambda qo=qo: lambda e: e.matmul(
            b.PS[5][:], h['E2'][:], b.bfslice(A_LOG + qo, 512, rows=8),
            start=True, stop=True))(),
            waits=[('ve', v_wfm), ('ve', getattr(b, 'v_wx2', 0))], inc='tn')
        tn_wx = s.cur('tn')
        s.op('ve', (lambda qo=qo, tn_wx=tn_wx: lambda e: e.tensor_copy(
            b.bfslice(A_WX2 + qo, 512), b.PS[5][:]))(),
            waits=[('tn', tn_wx)], inc='ve')
        b.v_wx2 = s.cur('ve')
    v_wx2 = s.cur('ve')

    afm = lambda dc: b.bfslice(A_AFM + dc * 128, 128)
    for dc in range(DC):
        for qi in range(NQT):
            s.op('tn', (lambda dc=dc, qi=qi: lambda e: e.matmul(
                b.PS[qi][:], afm(dc), X[:, dc * T + qi * 512:dc * T + qi * 512 + 512],
                start=(dc == 0), stop=(dc == DC - 1)))(),
                waits=[afm_toks[dc], ('ve', b.ps_free_stats), ('ve', b.v_log_free)],
                inc=None)
        s.op('tn', (lambda dc=dc: lambda e: e.matmul(
            b.PS[4][0:128, 0:1], afm(dc), h['ONEC'][:],
            start=(dc == 0), stop=(dc == DC - 1)))(),
            inc='tn' if dc == DC - 1 else None)
    tn_mid = s.cur('tn')
    s.op('ve', lambda e: e.tensor_copy(h['COLA'][:], b.PS[4][0:128, 0:1]),
         waits=[('tn', tn_mid)], inc='ve')
    t1 = b.f32slice(0, 512)
    for qi in range(NQT):
        qo = qi * 512
        s.op('ve', (lambda qi=qi, qo=qo, t1=t1: lambda e: e.scalar_tensor_tensor(
            t1[:], mean(qo), h['COLA'][:], b.PS[qi][:], ALU.mult, ALU.subtract))(),
            waits=[('ve', v_wx2)], inc='ve')
        s.op('ve', (lambda qo=qo, t1=t1: lambda e: e.tensor_tensor(
            b.bfslice(A_MIDW + qo, 512), t1[:], b.bfslice(A_WX2 + qo, 512),
            ALU.mult))(), inc='ve')
    v_midw = s.cur('ve')
    b.ps_free_mid = v_midw

    if getattr(b, 'deferred_prep', None):
        b.deferred_prep()
    fuse_prep = getattr(b, 'fuse_bw_prep', False)
    b.fuse_bw_prep = False

    WB = (h['WB0'], h['WB1'])
    wb_free = [list(b.wb_free[0]), list(b.wb_free[1])]
    fin = (b.bfslice(A_FIN, 512), b.bfslice(A_FIN + 512, 512))
    t2 = (b.f32slice(512, 512), b.f32slice(1024, 512))
    g_fin_free = [list(xw), list(xw)]
    last_evac_qh = [0, 0]
    fin_toks_by_oc = [[] for _ in range(DC)]
    dst_ap = dt[dst]
    prep_cast_free = []
    prep_bank_free = [('ve', s.cur('ve'))]
    prep_store_free = []
    wb_free_prep = [0]
    for oc in range(DC):
        if fuse_prep:
            wb = WB[1]
            ct = s.op('gp', (lambda oc=oc: lambda e: e.dma_start(
                out=WB[0][:], in_=dt['bw'][oc * 128:(oc + 1) * 128, :]))(),
                waits=prep_cast_free, inc='dgp')
            copy_toks = []
            for g in range(4):
                for slot in range(8):
                    dcq = g * 8 + slot
                    s.op('tn', (lambda dcq=dcq, slot=slot: lambda e: e.transpose(
                        b.PSB[:, slot * 128:(slot + 1) * 128],
                        WB[0][:, dcq * 128:(dcq + 1) * 128], h['IDENT'][:]))(),
                        waits=[ct] + prep_bank_free, inc='tn' if slot == 7 else None)
                    prep_bank_free = []
                tv = s.cur('tn')
                s.op('ve', (lambda g=g, tv=tv: lambda e: e.tensor_copy(
                    WB[1][:, g * 1024:(g + 1) * 1024], b.PSB[:]))(),
                    waits=[('tn', tv), ('tn', wb_free_prep[0]),
                           ('ve', b.ps_free_mid if oc == 0 else 0)] + prep_store_free,
                    inc='ve')
                prep_bank_free = [('ve', s.cur('ve'))]
            prep_cast_free = [('tn', s.cur('tn'))]
            vcp = s.cur('ve')
            bwc_toks = [('ve', vcp)]
            prep_store_free = [s.op('sy', (lambda oc=oc, vcp=vcp: lambda e: e.dma_start(
                out=dt['s_bwt'][oc * 128:(oc + 1) * 128, :], in_=WB[1][:]))(),
                waits=[('ve', vcp)], inc='dsy')]
        else:
            wb = WB[oc % 2]
            w = wb_free[oc % 2] + list(b.bwt_done)
            bwc_toks = [s.op('sy', (lambda oc=oc, wb=wb: lambda e: e.dma_start(
                out=wb[:], in_=dt['s_bwt'][oc * 128:(oc + 1) * 128, :]))(),
                waits=w, inc='dsy')]
        tn_oc_halves = [0, 0]
        for qh in range(2):
            bk = (0, 1) if qh == 0 else (2, 3)
            for dc in range(DC):
                lhs = wb[:, dc * 128:(dc + 1) * 128]
                for ql in range(2):
                    qi = qh * 2 + ql
                    s.op('tn', (lambda lhs=lhs, dc=dc, qi=qi, bk=bk, ql=ql: lambda e: e.matmul(
                        b.PS[bk[ql]][:], lhs, X[:, dc * T + qi * 512:dc * T + qi * 512 + 512],
                        start=(dc == 0), stop=False))(),
                        waits=bwc_toks + [('ve', b.ps_free_mid if oc == 0 else 0),
                                          ('ve', last_evac_qh[qh] if dc == 0 else 0)],
                        inc=None)
                if qh == 0:
                    s.op('tn', (lambda lhs=lhs, dc=dc: lambda e: e.matmul(
                        b.PS[4][0:128, 0:1], lhs, h['ONEC'][:],
                        start=(dc == 0), stop=(dc == DC - 1)))(), inc=None)
            for ql in range(2):
                qi = qh * 2 + ql
                s.op('tn', (lambda oc=oc, qi=qi, bk=bk, ql=ql: lambda e: e.matmul(
                    b.PS[bk[ql]][:], b.bfslice(A_BFM + oc * 128, 128),
                    b.bfslice(A_MIDW + qi * 512, 512), start=False, stop=True))(),
                    waits=[g_bfm, ('ve', v_midw)],
                    inc='tn' if ql == 1 else None)
            tn_oc_halves[qh] = s.cur('tn')
            if qh == 0:
                cw = h['COLW'][:, oc % 2:oc % 2 + 1]
                s.op('ve', (lambda cw=cw: lambda e: e.tensor_single_scalar(
                    cw, b.PS[4][0:128, 0:1], 0.5, ALU.mult))(),
                     waits=[('tn', tn_oc_halves[0])], inc='ve')
            for ql in range(2):
                qi = qh * 2 + ql
                qo = qi * 512
                par = (oc * NQT + qi) % 2
                s.op('ve', (lambda qi=qi, qo=qo, t1=t1, bk=bk, ql=ql: lambda e: e.scalar_tensor_tensor(
                    t1[:], b.PS[bk[ql]][:], 0.5, rsig(qo), ALU.mult, ALU.mult))(),
                    waits=[('tn', tn_oc_halves[qh])], inc='ve')
                s.op('ve', (lambda qo=qo, t1=t1, par=par, cw=cw: lambda e: e.scalar_tensor_tensor(
                    t2[par][:], nmsh(qo), cw, t1[:], ALU.mult, ALU.add))(), inc='ve')
                s.op('ve', (lambda oc=oc, qo=qo, par=par: lambda e: e.tensor_tensor(
                    fin[par][:], t2[par][:], X[:, oc * T + qo:oc * T + qo + 512],
                    ALU.add))(),
                    waits=g_fin_free[par], inc='ve')
                v_fin = s.cur('ve')
                tok = s.op('sy', (lambda oc=oc, qo=qo, par=par, v_fin=v_fin, dst_ap=dst_ap:
                            lambda e: e.dma_start(
                    out=dst_ap[oc * 128:(oc + 1) * 128, qo:qo + 512], in_=fin[par][:]))(),
                    waits=[('ve', v_fin)], inc='dsy')
                g_fin_free[par] = [tok]
                fin_toks_by_oc[oc].append(tok)
            last_evac_qh[qh] = s.cur('ve')
        wb_free[oc % 2] = [('tn', tn_oc_halves[1])]
        wb_free_prep[0] = tn_oc_halves[1]
    b.fin_stored = s.qall('dsy')
    b.fin_toks_by_oc = fin_toks_by_oc
    b.x_consumers = [('ve', s.cur('ve')), ('tn', s.cur('tn'))]
    if fuse_prep:
        b.bwt_done = s.qall('dsy')
        wb_free[0] = prep_cast_free
        wb_free[1] = [('tn', s.cur('tn'))] + prep_store_free
    b.wb_free = wb_free
    b.ps_free_stats = 0
    b.v_log_free = 0
    b.v_top2_done = 0
    b.v_wx2 = 0


def emit_attn_phase(b):
    emit_attn_proj(b)
    emit_attn_core(b)


def emit_attn_proj(b):
    s, h, dt = b.s, b.h, b.dt
    s.phase_barrier()
    X = h['X']
    A_RSIG, A_NMSH, A_MEAN = 0, 2048, 4096
    A_XSQ = 6144
    A_COS = 8192
    A_SIN = 10240
    A_QLN = 12288
    A_QRP = 14336
    rsig = lambda qo, n=512: b.bfslice(A_RSIG + qo, n)
    nmsh = lambda qo, n=512: b.bfslice(A_NMSH + qo, n)

    emit_stats(b, A_RSIG, A_NMSH, A_MEAN, A_XSQ)

    s.op('ve', lambda e: e.tensor_single_scalar(h['PCOL2'][:], h['PCOL'][:], 64.0, ALU.is_ge),
         waits=[('gp', b.g_pcol)], inc='ve')
    s.op('ve', lambda e: e.tensor_single_scalar(h['PCOL2'][:], h['PCOL2'][:], -64.0, ALU.mult),
         inc='ve')
    s.op('ve', lambda e: e.tensor_tensor(h['PCOL2'][:], h['PCOL'][:], h['PCOL2'][:], ALU.add),
         inc='ve')
    v_pm = s.cur('ve')
    s.op('sc', lambda e: e.activation(h['INVF'][:], h['PCOL2'][:], AF.Exp,
                                      scale=-math.log(THETA) / 64.0),
         waits=[('ve', v_pm)], inc='sc')
    sc_invf = s.cur('sc')
    b1 = b.f32slice(0, 512)
    b2 = b.f32slice(512, 512)
    b3 = b.f32slice(1024, 512)
    b4 = b.f32slice(1536, 512)
    ki = h['SCR_I32'][:, 0:512]
    for qq in range(4):
        base = qq * 512
        s.op('gp', (lambda base=base: lambda e: e.iota(
            b1[:], pattern=[[1, 512]], base=base, channel_multiplier=0,
            allow_small_or_imprecise_dtypes=True))(),
            waits=[('ve', getattr(b, 'v_rope_free', 0)),
                   ('ve', b.v_stats), ('sc', b.sc_stats)], inc='gp')
        g_t = s.cur('gp')
        s.op('ve', (lambda g_t=g_t: lambda e: e.tensor_single_scalar(
            b2[:], b1[:], h['INVF'][:], ALU.mult))(),
            waits=[('gp', g_t), ('sc', sc_invf)], inc='ve')
        s.op('ve', lambda e: e.tensor_single_scalar(b3[:], b2[:], INV_2PI, ALU.mult), inc='ve')
        s.op('ve', lambda e: e.tensor_copy(ki[:], b3[:]), inc='ve')
        s.op('ve', lambda e: e.tensor_copy(b1[:], ki[:]), inc='ve')
        s.op('ve', lambda e: e.tensor_tensor(b4[:], b3[:], b1[:], ALU.subtract), inc='ve')
        s.op('ve', lambda e: e.tensor_single_scalar(b4[:], b4[:], 0.5, ALU.is_ge), inc='ve')
        s.op('ve', lambda e: e.tensor_tensor(b1[:], b1[:], b4[:], ALU.add), inc='ve')
        s.op('ve', lambda e: e.tensor_single_scalar(b3[:], b3[:], 0.25, ALU.add), inc='ve')
        s.op('ve', lambda e: e.tensor_copy(ki[:], b3[:]), inc='ve')
        s.op('ve', lambda e: e.tensor_copy(b4[:], ki[:]), inc='ve')
        s.op('ve', lambda e: e.tensor_tensor(b3[:], b3[:], b4[:], ALU.subtract), inc='ve')
        s.op('ve', lambda e: e.tensor_single_scalar(b3[:], b3[:], 0.5, ALU.is_ge), inc='ve')
        s.op('ve', lambda e: e.tensor_tensor(b4[:], b4[:], b3[:], ALU.add), inc='ve')
        s.op('ve', lambda e: e.tensor_single_scalar(b4[:], b4[:], -0.25, ALU.add), inc='ve')
        s.op('ve', lambda e: e.scalar_tensor_tensor(
            b3[:], b1[:], -TWO_PI_HI, b2[:], ALU.mult, ALU.add), inc='ve')
        s.op('ve', lambda e: e.scalar_tensor_tensor(
            b1[:], b1[:], -TWO_PI_LO, b3[:], ALU.mult, ALU.add), inc='ve')
        v_ys = s.cur('ve')
        s.op('sc', (lambda: lambda e: e.activation(b3[:], b1[:], AF.Sin))(),
             waits=[('ve', v_ys)], inc='sc')
        sc_sin = s.cur('sc')
        s.op('ve', (lambda base=base, sc_sin=sc_sin: lambda e: e.tensor_single_scalar(
            b.bfslice(A_SIN + base, 512)[0:64, :], b3[0:64, :], -1.0, ALU.mult))(),
            waits=[('sc', sc_sin)], inc='ve')
        s.op('ve', (lambda base=base: lambda e: e.tensor_single_scalar(
            b.bfslice(A_SIN + base, 512)[64:128, :], b3[64:128, :], 1.0, ALU.mult))(),
            inc='ve')
        s.op('ve', lambda e: e.scalar_tensor_tensor(
            b3[:], b4[:], -TWO_PI_HI, b2[:], ALU.mult, ALU.add), inc='ve')
        s.op('ve', lambda e: e.scalar_tensor_tensor(
            b2[:], b4[:], -TWO_PI_LO, b3[:], ALU.mult, ALU.add), inc='ve')
        v_yc = s.cur('ve')
        s.op('sc', (lambda: lambda e: e.activation(b3[:], b2[:], AF.Sin))(),
             waits=[('ve', v_yc)], inc='sc')
        sc_cos = s.cur('sc')
        s.op('ve', (lambda base=base, sc_cos=sc_cos: lambda e: e.tensor_single_scalar(
            b.bfslice(A_COS + base, 512), b3[:], 1.0, ALU.mult))(),
            waits=[('sc', sc_cos)], inc='ve')
        b.v_rope_free = s.cur('ve')
    v_rope = s.cur('ve')

    WB = (h['WB0'], h['WB1'])
    wb_free = [list(b.wb_free[0]), list(b.wb_free[1])]
    wq_r = dt['wq'].rearrange("(dc p) (hc o) -> hc p dc o", p=128, o=128)
    wk_r = dt['wk'].rearrange("(dc p) (hc o) -> hc p dc o", p=128, o=128)
    wv_r = dt['wv'].rearrange("(dc p) (hc o) -> hc p dc o", p=128, o=128)
    qln = b.bfslice(A_QLN, 2048)
    qrp = b.bfslice(A_QRP, 2048)
    t1 = b.f32slice(0, 512)
    tr = b.f32slice(512, 512)
    v_spill_wait = [[]]

    v_proj_free_qh = [0, 0]

    def proj_head(idx, w_r, scr_dst, do_rope, ih):
        wb = WB[idx % 2]
        w_toks = []
        for piece in range(2):
            w_toks.append(s.op('gp', (lambda wb=wb, piece=piece: lambda e: e.dma_start(
                out=wb.ap().rearrange("p (c j) -> p c j", j=128)[:, piece * 16:(piece + 1) * 16],
                in_=w_r[ih][:, piece * 16:(piece + 1) * 16]))(),
                waits=wb_free[idx % 2] if piece == 0 else [], inc='dgp'))
        colh = h['COLH0'] if idx % 2 == 0 else h['COLH1']
        tn_halves = [0, 0]
        for qh in range(2):
            bk = (0, 1) if qh == 0 else (2, 3)
            for dc in range(DC):
                lhs = wb[:, dc * 128:(dc + 1) * 128]
                for ql in range(2):
                    qi = qh * 2 + ql
                    s.op('tn', (lambda lhs=lhs, dc=dc, qi=qi, bk=bk, ql=ql: lambda e: e.matmul(
                        b.PS[bk[ql]][:], lhs, X[:, dc * T + qi * 512:dc * T + qi * 512 + 512],
                        start=(dc == 0), stop=(dc == DC - 1)))(),
                        waits=w_toks + list(b.x_ready) +
                              [('ve', v_proj_free_qh[qh] if dc == 0 else 0)],
                        inc='tn' if (qh == 1 and dc == DC - 1 and ql == 1) else None)
                if qh == 0:
                    s.op('tn', (lambda lhs=lhs, dc=dc: lambda e: e.matmul(
                        b.PS[4][0:128, 0:1], lhs, h['ONEC'][:],
                        start=(dc == 0), stop=(dc == DC - 1)))(),
                        inc='tn' if dc == DC - 1 else None)
            tn_halves[qh] = s.cur('tn')
            if qh == 0:
                s.op('ve', (lambda colh=colh: lambda e: e.tensor_copy(
                    colh[:], b.PS[4][0:128, 0:1]))(),
                     waits=[('tn', tn_halves[0]), ('ve', b.v_stats), ('ve', v_rope)]
                           + v_spill_wait[0],
                     inc='ve')
            v_lns = {}
            for ql in range(2):
                qi = qh * 2 + ql
                qo = qi * 512
                s.op('ve', (lambda qo=qo, t1=t1, bk=bk, ql=ql: lambda e: e.tensor_tensor(
                    t1[:], b.PS[bk[ql]][:], rsig(qo), ALU.mult))(),
                    waits=[('tn', tn_halves[qh])] + (v_spill_wait[0] if qh == 1 else []),
                    inc='ve')
                s.op('ve', (lambda qo=qo, t1=t1, colh=colh: lambda e: e.scalar_tensor_tensor(
                    qln[:, qo:qo + 512], nmsh(qo), colh[:], t1[:], ALU.mult, ALU.add))(),
                    inc='ve')
                v_lns[ql] = s.cur('ve')
            if do_rope:
                tn_sws = {}
                for ql in range(2):
                    qo = (qh * 2 + ql) * 512
                    s.op('tn', (lambda qo=qo, bk=bk, ql=ql, v_ln=v_lns[ql]:
                                lambda e: e.matmul(
                        b.PS[bk[ql]][:], h['PSW'][:], qln[:, qo:qo + 512],
                        start=True, stop=True))(),
                        waits=[('ve', v_lns[ql])], inc='tn')
                    tn_sws[ql] = s.cur('tn')
                for ql in range(2):
                    qo = (qh * 2 + ql) * 512
                    s.op('ve', (lambda qo=qo, t1=t1: lambda e: e.tensor_tensor(
                        t1[:], qln[:, qo:qo + 512], b.bfslice(A_COS + qo, 512), ALU.mult))(),
                        inc='ve')
                    s.op('ve', (lambda qo=qo, bk=bk, ql=ql:
                                lambda e: e.tensor_tensor(
                        tr[:], b.PS[bk[ql]][:], b.bfslice(A_SIN + qo, 512), ALU.mult))(),
                        waits=[('tn', tn_sws[ql])], inc='ve')
                    s.op('ve', (lambda qo=qo, t1=t1, tr=tr: lambda e: e.tensor_tensor(
                        qrp[:, qo:qo + 512], t1[:], tr[:], ALU.add))(), inc='ve')
            v_proj_free_qh[qh] = s.cur('ve')
        v_done = s.cur('ve')
        src = qrp if do_rope else qln
        tok = s.op('sy', (lambda src=src, scr_dst=scr_dst: lambda e: e.dma_start(
            out=scr_dst, in_=src[:]))(),
            waits=[('ve', v_done)], inc='dsy')
        v_spill_wait[0] = [tok]
        wb_free[idx % 2] = [('tn', tn_halves[1])]

    idx = 0
    for H in range(NKV):
        proj_head(idx, wk_r, dt['s_k'][H * 128:(H + 1) * 128, :], True, H)
        idx += 1
    for H in range(NKV):
        proj_head(idx, wv_r, dt['s_v'][H * 128:(H + 1) * 128, :], False, H)
        idx += 1
    v_spilled = s.qall('dsy')
    vst_free = [[], []]
    for tc in range(TC):
        base_off = 16384 if tc % 2 == 0 else 18432
        xb_toks = []
        for H in range(NKV):
            xb_toks.append(s.op('sc', (lambda H=H, tc=tc, base_off=base_off:
                                       lambda e: e.dma_start_transpose(
                b.bfslice(base_off + H * 128, 128),
                dt['s_v'][H * 128:(H + 1) * 128, tc * 128:(tc + 1) * 128]))(),
                waits=(v_spilled if (tc == 0 and H == 0) else []) + vst_free[tc % 2],
                inc='dsc'))
        st = s.op('sc', (lambda tc=tc, base_off=base_off: lambda e: e.dma_start(
            out=dt['s_vt'][tc * 128:(tc + 1) * 128, :],
            in_=b.bfslice(base_off, 1024)))(),
            waits=xb_toks, inc='dsc')
        vst_free[tc % 2] = [st]
    for hh in range(NH):
        proj_head(idx, wq_r, dt['s_q'][hh * 128:(hh + 1) * 128, :], True, hh)
        idx += 1
    b.spills_done = s.qall('dsy') + s.qall('dsc')
    b.wb_free = wb_free


def emit_attn_core(b):
    s, h, dt = b.s, b.h, b.dt
    s.phase_barrier()
    X = h['X']
    A_CTX = 0
    A_KW = (16384, 17920)
    A_VW = (19456, 20992)
    A_QB = 22528
    A_EX = 23552
    recip = b.f32slice(0, 512)
    recip2 = b.f32slice(512, 512)
    b.v_ctx_evac = [0, 0]
    s.op('ve', lambda e: e.tensor_single_scalar(
        h['MASK1'][:], h['IOTA'][:, 0:128], 0.0, ALU.is_ge), inc='ve')
    s.op('ve', lambda e: e.tensor_single_scalar(
        h['MASK2'][:], h['IOTA'][:, 0:128], -1.0, ALU.is_le), inc='ve')
    v_masks = s.cur('ve')

    wo_r = dt['s_wo'].rearrange("(oc p) (dc j) -> dc p oc j", p=128, j=128)
    vt_r = dt['s_vt'].rearrange("(kc p) (H j) -> H p kc j", p=128, j=128)
    WB = (h['WB0'], h['WB1'])
    wb_free = [list(b.wb_free[0]), list(b.wb_free[1])]
    kv_free = [[], []]
    qb_free = [[], []]
    ex_free = [0, 0, 0]
    ctx_free = [0]
    kv_toks_buf = [[], []]
    b.ex_rot = 0

    def load_kv(qidx, Hkv, par):
        q0 = qidx * 512
        kf = max(0, -(-(q0 - 1151) // 128))
        kl = (q0 + 511) // 128
        nkc = kl - kf + 1
        toks = []
        toks.append(s.op('sy', (lambda Hkv=Hkv, kf=kf, nkc=nkc, par=par: lambda e: e.dma_start(
            out=b.bfslice(A_KW[par], nkc * 128),
            in_=dt['s_k'][Hkv * 128:(Hkv + 1) * 128, kf * 128:kf * 128 + nkc * 128]))(),
            waits=kv_free[par] + list(b.spills_done), inc='dsy'))
        toks.append(s.op('sy', (lambda Hkv=Hkv, kf=kf, nkc=nkc, par=par: lambda e: e.dma_start(
            out=b.bfslice(A_VW[par], nkc * 128).rearrange("p (c j) -> p c j", j=128),
            in_=vt_r[Hkv][:, kf:kf + nkc, :]))(),
            waits=kv_free[par], inc='dsy'))
        kv_toks_buf[par] = toks

    load_kv(0, 0, 0)
    for qi in range(NQT):
        q0 = qi * 512
        kf = max(0, -(-(q0 - 1151) // 128))
        kl = (q0 + 511) // 128
        nkc = kl - kf + 1
        for Hkv in range(NKV):
            par = (qi * NKV + Hkv) % 2
            kv_toks = kv_toks_buf[par]
            nxt = qi * NKV + Hkv + 1
            if nxt < NQT * NKV:
                load_kv(nxt // NKV, nxt % NKV, (par + 1) % 2)
            for g2 in range(0, GQ, 2):
                heads = [Hkv * GQ + g2, Hkv * GQ + g2 + 1]
                g_q = {}
                for sub in (0, 1):
                    hh = heads[sub]
                    g_q[sub] = s.op('sy', (lambda hh=hh, q0=q0, sub=sub: lambda e: e.dma_start(
                        out=b.bfslice(A_QB + sub * 512, 512),
                        in_=dt['s_q'][hh * 128:(hh + 1) * 128, q0:q0 + 512]))(),
                        waits=qb_free[sub] + list(b.spills_done), inc='dsy')
                ps_ctx = (2, 5)
                ps_den = (3, 6)
                exp_done = {0: {}, 1: {}}
                mask_done = {0: {}, 1: {}}
                exslot = {}
                tn_last_den = [0, 0]
                tn_last_score = [0, 0]
                for ci in range(nkc):
                    kc = kf + ci
                    delta = q0 - kc * 128
                    if -384 <= delta <= 0:
                        lo2, hi2 = -delta, 512
                        mlo, mk = lo2, 'MASK1'
                    elif 640 <= delta <= 1024:
                        lo2, hi2 = 0, (1024 - delta) + 128
                        mlo, mk = hi2 - 128, 'MASK2'
                    else:
                        lo2, hi2 = 0, 512
                        mlo, mk = None, None
                    slots = {}
                    for sub in (0, 1):
                        sbank = (0 if ci % 2 == 0 else 4) if sub == 0 else 1
                        sdep = 2 if sub == 0 else 1
                        s.op('tn', (lambda ci=ci, sub=sub, par=par, lo2=lo2, hi2=hi2,
                                    sbank=sbank: lambda e: e.matmul(
                            b.PS[sbank][:, lo2:hi2], b.bfslice(A_KW[par] + ci * 128, 128),
                            b.bfslice(A_QB + sub * 512, 512)[:, lo2:hi2],
                            start=True, stop=True))(),
                            waits=[g_q[sub]] + kv_toks + [('ve', v_masks),
                                   ('sc', exp_done[sub].get(ci - sdep, 0))], inc='tn')
                        tn_s = s.cur('tn')
                        tn_last_score[sub] = tn_s
                        slot = b.ex_rot
                        b.ex_rot = (slot + 1) % 3
                        slots[sub] = slot
                        exb = b.bfslice(A_EX + slot * 512, 512)
                        exslot[(sub, ci)] = exb
                        s.op('sc', (lambda exb=exb, tn_s=tn_s, lo2=lo2, hi2=hi2,
                                    sbank=sbank: lambda e: e.activation(
                            exb[:, lo2:hi2], b.PS[sbank][:, lo2:hi2], AF.Exp,
                            bias=h['CEXP'][:], scale=SM))(),
                            waits=[('tn', tn_s), ('gp', b.g_cexp),
                                   ('tn', ex_free[slot])], inc='sc')
                        exp_done[sub][ci] = s.cur('sc')
                        vm = 0
                        if mk is not None:
                            s.op('gp', (lambda exb=exb, mlo=mlo, mk=mk: lambda e: e.tensor_tensor(
                                exb[:, mlo:mlo + 128], exb[:, mlo:mlo + 128],
                                h[mk][:], ALU.mult))(),
                                waits=[('sc', exp_done[sub][ci])], inc='gp')
                            vm = s.cur('gp')
                        mask_done[sub][ci] = vm
                    for sub in (0, 1):
                        exb = exslot[(sub, ci)]
                        vm = mask_done[sub][ci]
                        plo, phi = lo2, hi2
                        if ci == 0 and (lo2 > 0 or hi2 < 512):
                            if hi2 < 512:
                                s.op('gp', (lambda exb=exb, hi2=hi2: lambda e: e.memset(
                                    exb[:, hi2:512], 0.0))(),
                                    waits=[('sc', exp_done[sub][ci]), ('gp', vm)],
                                    inc='gp')
                            if lo2 > 0:
                                s.op('gp', (lambda exb=exb, lo2=lo2: lambda e: e.memset(
                                    exb[:, 0:lo2], 0.0))(),
                                    waits=[('sc', exp_done[sub][ci]), ('gp', vm)],
                                    inc='gp')
                            vm = s.cur('gp')
                            plo, phi = 0, 512
                        s.op('tn', (lambda ci=ci, exb=exb, first=(ci == 0),
                                    last=(ci == nkc - 1), vm=vm, par=par,
                                    pc=ps_ctx[sub], plo=plo, phi=phi: lambda e: e.matmul(
                            b.PS[pc][:, plo:phi], b.bfslice(A_VW[par] + ci * 128, 128),
                            exb[:, plo:phi], start=first, stop=last))(),
                            waits=[('sc', exp_done[sub][ci]), ('gp', vm),
                                   ('ve', b.v_ctx_evac[sub] if ci == 0 else 0)],
                            inc=None)
                        s.op('tn', (lambda ci=ci, exb=exb, first=(ci == 0),
                                    last=(ci == nkc - 1), pd=ps_den[sub],
                                    plo=plo, phi=phi: lambda e: e.matmul(
                            b.PS[pd][:, plo:phi], h['ONES'][:], exb[:, plo:phi],
                            start=first, stop=last))(),
                            inc='tn')
                        ex_free[slots[sub]] = s.cur('tn')
                        tn_last_den[sub] = s.cur('tn')
                for sub in (0, 1):
                    hh = heads[sub]
                    qb_free[sub] = [('tn', tn_last_score[sub])]
                    rc = recip if sub == 0 else recip2
                    s.op('ve', (lambda rc=rc, pd=ps_den[sub]:
                                lambda e: e.reciprocal(rc[:], b.PS[pd][:]))(),
                        waits=[('tn', tn_last_den[sub])], inc='ve')
                    s.op('ve', (lambda hh=hh, rc=rc, pc=ps_ctx[sub]: lambda e: e.tensor_tensor(
                        b.bfslice(A_CTX + hh * 512, 512), b.PS[pc][:], rc[:], ALU.mult))(),
                        waits=[('tn', ctx_free[0])], inc='ve')
                    b.v_ctx_evac[sub] = s.cur('ve')
            kv_free[par] = [('tn', s.cur('tn'))]
        v_ctx_ready = s.cur('ve')
        last_x_evac = 0
        for dc in range(DC):
            wb = WB[dc % 2]
            wo_toks = []
            for piece in range(2):
                wo_toks.append(s.op('sy', (lambda dc=dc, piece=piece, wb=wb: lambda e: e.dma_start(
                    out=wb.ap().rearrange("p (c j) -> p c j", j=128)[:, piece * 16:(piece + 1) * 16],
                    in_=wo_r[dc][:, piece * 16:(piece + 1) * 16]))(),
                    waits=(wb_free[dc % 2] + list(b.wo_prep)) if piece == 0 else [], inc='dsy'))
            pso = 2 if dc % 2 == 0 else 5
            for ocp in range(DC):
                s.op('tn', (lambda ocp=ocp, wb=wb, pso=pso: lambda e: e.matmul(
                    b.PS[pso][:], wb[:, ocp * 128:(ocp + 1) * 128],
                    b.bfslice(A_CTX + ocp * 512, 512),
                    start=(ocp == 0), stop=(ocp == DC - 1)))(),
                    waits=wo_toks + [('ve', v_ctx_ready),
                                     ('ve', last_x_evac if ocp == 0 else 0)],
                    inc='tn' if ocp == DC - 1 else None)
            tn_o = s.cur('tn')
            wb_free[dc % 2] = [('tn', tn_o)]
            s.op('ve', (lambda dc=dc, q0=q0, pso=pso, tn_o=tn_o: lambda e: e.tensor_tensor(
                X[:, dc * T + q0:dc * T + q0 + 512], b.PS[pso][:],
                X[:, dc * T + q0:dc * T + q0 + 512], ALU.add))(),
                waits=[('tn', tn_o)], inc='ve')
            if dc % 2 == 1 or dc == DC - 1:
                last_x_evac = s.cur('ve')
        ctx_free[0] = s.cur('tn')
        b.v_ctx_evac = [s.cur('ve'), s.cur('ve')]
    b.x_ready = []
    b.wb_free = wb_free
    b.fin_stored = []


def emit_output(b, src):
    s, h, dt = b.s, b.h, b.dt
    s.phase_barrier()
    WB = (h['WB0'], h['WB1'])
    wb_free = [list(b.wb_free[0]), list(b.wb_free[1])]
    for tc in range(TC):
        wb = WB[tc % 2]
        g_tr = s.op('sy', (lambda tc=tc, wb=wb, src=src: lambda e: e.dma_start_transpose(
            wb[:], dt[src][:, tc * 128:(tc + 1) * 128]))(),
            waits=wb_free[tc % 2] + list(b.fin_stored), inc='dsy')
        g_out = s.op('gp', (lambda tc=tc, wb=wb: lambda e: e.dma_start(
            out=dt['out'][tc * 128:(tc + 1) * 128, :], in_=wb[:]))(),
            waits=[g_tr], inc='dgp')
        wb_free[tc % 2] = [g_out]
    s.barrier_wait('gp', s.qall('dgp'))


_NC_CACHE = {}


def _get_nc(phases):
    key = tuple(phases)
    if key not in _NC_CACHE:
        _NC_CACHE[key] = build(phases)
    return _NC_CACHE[key]


def kernel(L0_attn_x, L0_attn_wq, L0_attn_wk, L0_attn_wv, L0_attn_wo,
           L0_ffn_base_W, L0_ffn_router_W, L0_ffn_lora_A, L0_ffn_lora_B):
    nc = _get_nc(PHASES)
    base = {
        "wq": np.ascontiguousarray(L0_attn_wq),
        "wk": np.ascontiguousarray(L0_attn_wk),
        "wv": np.ascontiguousarray(L0_attn_wv),
        "wo": np.ascontiguousarray(L0_attn_wo),
        "bw": np.ascontiguousarray(L0_ffn_base_W),
        "rw": np.ascontiguousarray(L0_ffn_router_W),
        "la": np.ascontiguousarray(L0_ffn_lora_A),
        "lb": np.ascontiguousarray(L0_ffn_lora_B),
    }
    in_maps = []
    for bb in range(8):
        m = dict(base)
        m["x"] = np.ascontiguousarray(L0_attn_x[bb])
        in_maps.append(m)
    res = run_bass_kernel_spmd(nc, in_maps, core_ids=list(range(8)))
    return np.stack([np.asarray(r["out"]) for r in res.results], axis=0)
